# version 10
# baseline (speedup 1.0000x reference)
import contextlib
import numpy as np
import ml_dtypes
import concourse.bass as bass
import concourse.mybir as mybir
from concourse.bass_utils import run_bass_kernel_spmd

F32 = mybir.dt.float32
BF16 = mybir.dt.bfloat16
I32 = mybir.dt.int32
AF = mybir.ActivationFunctionType
AL = mybir.AluOpType
AX = mybir.AxisListType

B = 2
N = 8192
D = 1024
H = 16
DH = 64
HID = 3072
NPAIR = H // 2
CH = 256
NSLOT = 8
OWN = NSLOT * CH
NCT = D // 128
NHT = HID // 128
EPS = 1e-5
QSCALE = 1.0 / 64.0
DQ = 1.0 / 127.0
SHIFT_MARGIN = 30.0

ESZ = {F32: 4, BF16: 2, I32: 4}

PARANOID = False
PARANOID_PHASES = None
PHASES = 99
P4_BG = True
DBG = set()
P3_SLOTS = 99
P3_GROUPS = 4
P3_PARTS = {"sim", "exp", "mask", "pv", "rs", "norm"}


class _Op:
    __slots__ = ("idx", "eng", "kind", "emit", "deps", "inc", "count", "waits",
                 "phase")

    def __init__(self, idx, eng, kind, emit, deps):
        self.phase = 0
        self.idx = idx
        self.eng = eng
        self.kind = kind
        self.emit = emit
        self.deps = deps
        self.inc = kind == "d"
        self.count = None
        self.waits = []


class Em:
    def __init__(self):
        self.ops = []
        self.wstate = {}
        self.rstate = {}
        self.cur_phase = 0

    @staticmethod
    def _overlap(a, b):
        return a[1] > b[0] and b[1] > a[0] and a[3] > b[2] and b[3] > a[2]

    def _scan(self, state, acc):
        lst = state.get(acc[0])
        if not lst:
            return ()
        r = acc[1:]
        return tuple(e[4] for e in lst if self._overlap(r, e))

    def add(self, eng, kind, emit, ins=(), outs=()):
        idx = len(self.ops)
        deps = set()
        for acc in ins:
            deps.update(self._scan(self.wstate, acc))
        for acc in outs:
            deps.update(self._scan(self.wstate, acc))
            deps.update(self._scan(self.rstate, acc))
        deps.discard(idx)
        for acc in ins:
            self.rstate.setdefault(acc[0], []).append(
                (acc[1], acc[2], acc[3], acc[4], idx))
        for acc in outs:
            key, p0, p1, b0, b1 = acc
            for state in (self.wstate, self.rstate):
                lst = state.get(key)
                if lst:
                    lst[:] = [e for e in lst
                              if not (p0 <= e[0] and e[1] <= p1
                                      and b0 <= e[2] and e[3] <= b1)]
            self.wstate.setdefault(key, []).append((p0, p1, b0, b1, idx))
        op = _Op(idx, eng, kind, emit, deps)
        op.phase = self.cur_phase
        self.ops.append(op)
        return op

    def resolve(self):
        ops = self.ops
        for op in ops:
            for d in op.deps:
                dop = ops[d]
                if dop.eng == "tensor" and op.eng == "tensor" \
                        and dop.kind == "c" and op.kind == "c":
                    continue
                dop.inc = True
        counters = {}
        for op in ops:
            if op.inc:
                key = (op.kind, op.eng)
                counters[key] = counters.get(key, 0) + (16 if op.kind == "d" else 1)
                op.count = counters[key]
        stream_guar = {}
        last_count = {}
        for op in ops:
            base = stream_guar.setdefault(op.eng, {})
            needed = {}
            if PARANOID and (PARANOID_PHASES is None
                             or op.phase in PARANOID_PHASES):
                for key, val in last_count.items():
                    if key[1] != op.eng or key[0] != op.kind:
                        needed[key] = val
            for d in op.deps:
                dop = ops[d]
                if dop.eng == "tensor" and op.eng == "tensor" \
                        and dop.kind == "c" and op.kind == "c":
                    continue
                key = (dop.kind, dop.eng)
                if needed.get(key, -1) < dop.count:
                    needed[key] = dop.count
            if op.inc:
                last_count[(op.kind, op.eng)] = op.count
            waits = []
            for key, val in sorted(needed.items()):
                if base.get(key, -1) < val:
                    waits.append((key, val))
                    base[key] = val
            op.waits = waits
        return counters


SB_BASE = 24576


class SBuf:
    def __init__(self, nc, name, cols, dtype, off):
        off = off + SB_BASE
        self.h = nc.alloc_sbuf_tensor_at(name, [128, cols], dtype, offset=off)
        self.esz = ESZ[dtype]
        self.off = off
        self.cols = cols

    def v(self, p0=0, p1=128, c0=0, c1=None):
        if c1 is None:
            c1 = self.cols
        ap = self.h[p0:p1, c0:c1]
        acc = ("SB", p0, p1, self.off + c0 * self.esz, self.off + c1 * self.esz)
        return ap, acc

    def vr(self, ap, p0, p1, c0, c1):
        return ap, ("SB", p0, p1, self.off + c0 * self.esz, self.off + c1 * self.esz)


class DBuf:
    def __init__(self, nc, name, rows, cols, dtype, kind="Internal"):
        if kind == "Internal" and name in DBG:
            kind = "ExternalOutput"
        self.h = nc.dram_tensor(name, [rows, cols], dtype, kind=kind)
        self.esz = ESZ[dtype]
        self.name = name
        self.rows = rows
        self.cols = cols

    def v(self, r0=0, r1=None, c0=0, c1=None):
        if r1 is None:
            r1 = self.rows
        if c1 is None:
            c1 = self.cols
        ap = self.h.ap()[r0:r1, c0:c1]
        acc = ("DR:" + self.name, r0, r1, c0 * self.esz, c1 * self.esz)
        return ap, acc


def build_program():
    nc = bass.Bass("TRN2", target_bir_lowering=False)
    em = Em()

    x_full = DBuf(nc, "x_full", N, D, F32, kind="ExternalInput")
    x_own = DBuf(nc, "x_own", OWN, D, F32, kind="ExternalInput")
    wq_in = DBuf(nc, "wq", D, D, F32, kind="ExternalInput")
    wkv_in = DBuf(nc, "wkv", 2 * DH, D, F32, kind="ExternalInput")
    lngT_in = DBuf(nc, "lngT", 128, 8, F32, kind="ExternalInput")
    lnbT_in = DBuf(nc, "lnbT", 128, 8, F32, kind="ExternalInput")
    wout_in = DBuf(nc, "wout", D, D, F32, kind="ExternalInput")
    lnog_in = DBuf(nc, "lnog", 128, D, F32, kind="ExternalInput")
    lnob_in = DBuf(nc, "lnob", 128, D, F32, kind="ExternalInput")
    wg_in = DBuf(nc, "wg", HID, D, I32, kind="ExternalInput")
    wu_in = DBuf(nc, "wu", HID, D, I32, kind="ExternalInput")
    wd_in = DBuf(nc, "wd", D, HID, I32, kind="ExternalInput")
    mask_in = DBuf(nc, "maskt", 8 * 128, CH, BF16, kind="ExternalInput")
    y_own = DBuf(nc, "y_own", OWN, D, F32, kind="ExternalOutput")

    sp_xn = DBuf(nc, "sp_xn", N, D, BF16)
    sp_xno = DBuf(nc, "sp_xno", OWN, D, BF16)
    sp_wq = DBuf(nc, "sp_wq", D, D, BF16)
    sp_wkv = DBuf(nc, "sp_wkv", 2 * DH, D, BF16)
    sp_wout = DBuf(nc, "sp_wout", D, D, BF16)
    sp_wg = DBuf(nc, "sp_wg", HID, D, BF16)
    sp_wu = DBuf(nc, "sp_wu", HID, D, BF16)
    sp_wd = DBuf(nc, "sp_wd", D, HID, BF16)
    sp_x1 = DBuf(nc, "sp_x1", OWN, D, F32)
    sp_x1n = DBuf(nc, "sp_x1n", OWN, D, BF16)
    sp_misc = DBuf(nc, "sp_misc", 8, 8, F32)
    dbg_ot = DBuf(nc, "dbg_ot", 128, NPAIR * OWN, BF16)
    dbg_kt = DBuf(nc, "dbg_kt", 128, N, BF16)
    dbg_va = DBuf(nc, "dbg_va", 128, 4096, BF16)
    dbg_km = DBuf(nc, "dbg_km", 128, 16, F32)
    dbg_bt = DBuf(nc, "dbg_bt", 128, 16, F32)
    dbg_sa = DBuf(nc, "dbg_sa", 128, 32, F32)

    maskS = SBuf(nc, "maskS", 8 * CH, BF16, 0)
    biasT = SBuf(nc, "biasT", 16, F32, 4096)
    ones64 = SBuf(nc, "ones64", 128, BF16, 4160)
    onesrow = SBuf(nc, "onesrow", 128, BF16, 4416)
    epsT = SBuf(nc, "epsT", 8, F32, 4672)
    kmaxsq = SBuf(nc, "kmaxsq", 16, F32, 4704)
    smallA = SBuf(nc, "smallA", 32, F32, 4768)
    bcolB = SBuf(nc, "bcolB", 8, BF16, 4896)
    stats = SBuf(nc, "stats", 80, F32, 4928)
    bias2 = SBuf(nc, "bias2", 96, F32, 203168)
    lnbTb = SBuf(nc, "lnbTb", 8, BF16, 203552)
    PTb = SBuf(nc, "PTb", 2048, BF16, 5536)
    xbuf = SBuf(nc, "xbuf", 2048, F32, 9632)
    scr = SBuf(nc, "scr", 4096, F32, 17824)
    etile = SBuf(nc, "etile", 2048, BF16, 34208)
    lnoG = SBuf(nc, "lnoG", D, F32, 39328)
    lnoB = SBuf(nc, "lnoB", D, F32, 43424)
    vaug = SBuf(nc, "vaug", 4096, BF16, 47520)
    wqT = SBuf(nc, "wqT", 8 * D, BF16, 55712)
    wkvT = SBuf(nc, "wkvT", 8 * 128, BF16, 72096)
    xnTb = SBuf(nc, "xnTb", 8 * 1024, BF16, 74144)
    wgT = SBuf(nc, "wgT", 8 * HID, BF16, 55712)
    qT = SBuf(nc, "qT", NPAIR * OWN, BF16, 104864)
    kT2 = SBuf(nc, "kT2", N, BF16, 137632)
    wuT = SBuf(nc, "wuT", 8 * HID, BF16, 104864)
    oT = SBuf(nc, "oT", NPAIR * OWN, BF16, 154016)
    woutT = SBuf(nc, "woutT", 8 * D, BF16, 186784)
    wdT = SBuf(nc, "wdT", NHT * D, BF16, 154016)
    hsbH = SBuf(nc, "hsbH", NHT * CH, BF16, 34208)

    psum_ctx = contextlib.ExitStack()
    ps_h = psum_ctx.enter_context(nc.psum_tensor("ps", [128, 4096], F32))

    def PSv(p0, p1, c0, c1):
        ap = ps_h[p0:p1, c0:c1]
        b0 = (c0 // 512) * 2048
        b1 = ((c1 + 511) // 512) * 2048
        return ap, ("PS", 0, 128, b0, b1)

    def mm(out_v, lhs_v, rhs_v, start, stop, tp=None):
        (oap, oacc) = out_v
        (lap, lacc) = lhs_v
        (rap, racc) = rhs_v
        kw = {}
        if tp is not None:
            kw["tile_position"] = tp
        em.add("tensor", "c",
               lambda e, oap=oap, lap=lap, rap=rap, kw=kw, start=start, stop=stop:
               e.matmul(oap, lap, rap, start=start, stop=stop, **kw),
               ins=[lacc, racc], outs=[oacc])

    def act(out_v, in_v, func, bias=0.0, scale=1.0):
        (oap, oacc) = out_v
        (iap, iacc) = in_v
        ins = [iacc]
        bap = bias
        if isinstance(bias, tuple):
            bap = bias[0]
            ins.append(bias[1])
        em.add("scalar", "c",
               lambda e, oap=oap, iap=iap, func=func, bap=bap, scale=scale:
               e.activation(oap, iap, func, bias=bap, scale=scale),
               ins=ins, outs=[oacc])

    def v_op(fn_name, out_v, *in_vs, **kw):
        (oap, oacc) = out_v
        iaps = [iv[0] for iv in in_vs]
        iaccs = [iv[1] for iv in in_vs]

        def emit(e, fn_name=fn_name, oap=oap, iaps=iaps, kw=kw):
            return getattr(e, fn_name)(oap, *iaps, **kw)
        em.add("vector", "c", emit, ins=iaccs, outs=[oacc])

    def v_ts(out_v, in_v, s1, s2, op0, op1=None):
        (oap, oacc) = out_v
        (iap, iacc) = in_v
        ins = [iacc]
        s1ap, s2ap = s1, s2
        if isinstance(s1, tuple):
            s1ap = s1[0]
            ins.append(s1[1])
        if isinstance(s2, tuple):
            s2ap = s2[0]
            ins.append(s2[1])
        kw = {}
        if op1 is not None:
            kw["op1"] = op1
        em.add("vector", "c",
               lambda e, oap=oap, iap=iap, s1ap=s1ap, s2ap=s2ap, op0=op0, kw=kw:
               e.tensor_scalar(oap, iap, s1ap, s2ap, op0, **kw),
               ins=ins, outs=[oacc])

    def v_tt(out_v, a_v, b_v, op):
        v_op("tensor_tensor", out_v, a_v, b_v, op=op)

    def dma(queue, out_v, in_v, transpose=False):
        (oap, oacc) = out_v
        (iap, iacc) = in_v
        if transpose:
            em.add(queue, "d",
                   lambda e, oap=oap, iap=iap: e.dma_start_transpose(oap, iap),
                   ins=[iacc], outs=[oacc])
        else:
            em.add(queue, "d",
                   lambda e, oap=oap, iap=iap: e.dma_start(out=oap, in_=iap),
                   ins=[iacc], outs=[oacc])

    _tq = [0]

    def tq():
        _tq[0] ^= 1
        return "scalar"

    def memset(out_v, val):
        (oap, oacc) = out_v
        em.add("vector", "c", lambda e, oap=oap, val=val: e.memset(oap, val),
               ins=[], outs=[oacc])

    memset(epsT.v(0, 128, 0, 1), EPS)
    memset(epsT.v(0, 128, 1, 2), EPS * EPS)
    memset(ones64.v(0, 128, 0, 64), 1.0)
    memset(ones64.v(0, 128, 64, 128), 1.0)
    memset(onesrow.v(0, 1, 0, 128), 1.0)

    for r in range(8):
        dma("gpsimd", maskS.v(0, 128, r * CH, (r + 1) * CH),
            mask_in.v(r * 128, (r + 1) * 128, 0, CH))
    dma("gpsimd", bias2.v(0, 128, 0, 8), lngT_in.v())
    dma("gpsimd", bias2.v(0, 128, 8, 16), lnbT_in.v())
    dma("gpsimd", lnoG.v(), lnog_in.v())
    dma("gpsimd", lnoB.v(), lnob_in.v())

    for et in range(8):
        te = et % 2
        wt_f = scr.v(0, 128, te * 1024, (te + 1) * 1024)
        dma("sync", wt_f, wq_in.v(et * 128, (et + 1) * 128, 0, D))
        stg = PTb.v(0, 128, te * 1024, (te + 1) * 1024)
        v_ts(stg, wt_f, QSCALE, None, AL.mult)
        dma("sync", sp_wq.v(et * 128, (et + 1) * 128, 0, D), stg)
    for ct in range(8):
        dma("scalar", wqT.v(0, 128, ct * 1024, (ct + 1) * 1024),
            sp_wq.v(0, D, ct * 128, (ct + 1) * 128), transpose=True)

    dma("gpsimd", sp_wkv.v(), wkv_in.v())
    for ct in range(8):
        dma("scalar", wkvT.v(0, 128, ct * 128, (ct + 1) * 128),
            sp_wkv.v(0, 2 * DH, ct * 128, (ct + 1) * 128), transpose=True)

    dma("gpsimd", sp_wout.v(), wout_in.v())

    v_op("tensor_copy", lnbTb.v(0, 128, 0, 8), bias2.v(0, 128, 8, 16))
    for et in range(8):
        for ct in range(8):
            mm(PSv(0, 128, 3072 + et, 3073 + et),
               wqT.v(0, 128, ct * 1024 + et * 128, ct * 1024 + (et + 1) * 128),
               lnbTb.vr(lnbTb.h[0:128, ct:ct + 1], 0, 128, ct, ct + 1),
               start=(ct == 0), stop=(ct == 7))
    for ct in range(8):
        mm(PSv(0, 128, 3080, 3081),
           wkvT.v(0, 128, ct * 128, (ct + 1) * 128),
           lnbTb.vr(lnbTb.h[0:128, ct:ct + 1], 0, 128, ct, ct + 1),
           start=(ct == 0), stop=(ct == 7))
        mm(PSv(64, 128, 3081, 3082),
           wkvT.v(0, 128, ct * 128, ct * 128 + 64),
           lnbTb.vr(lnbTb.h[0:128, ct:ct + 1], 0, 128, ct, ct + 1),
           start=(ct == 0), stop=(ct == 7), tp=(0, 64))
    v_op("tensor_copy", bias2.v(0, 128, 16, 26), PSv(0, 128, 3072, 3082))
    for ct in range(8):
        mm(PSv(0, 1, 3584, 3712),
           lnbTb.vr(lnbTb.h[0:128, ct:ct + 1], 0, 128, ct, ct + 1),
           wkvT.v(0, 128, ct * 128, (ct + 1) * 128),
           start=(ct == 0), stop=(ct == 7))
    v_op("tensor_copy", etile.v(0, 1, 0, 128), PSv(0, 1, 3584, 3712))
    mm(PSv(0, 128, 3712, 3776), onesrow.v(0, 1, 0, 128), etile.v(0, 1, 64, 128),
       start=True, stop=True)
    v_op("tensor_copy", bias2.v(0, 128, 32, 96), PSv(0, 128, 3712, 3776))
    for ct in range(8):
        gcol = (bias2.h[0:128, ct:ct + 1], bias2.v(0, 128, ct, ct + 1)[1])
        v_ts(wqT.v(0, 128, ct * 1024, (ct + 1) * 1024),
             wqT.v(0, 128, ct * 1024, (ct + 1) * 1024), gcol, None, AL.mult)
        v_ts(wkvT.v(0, 128, ct * 128, (ct + 1) * 128),
             wkvT.v(0, 128, ct * 128, (ct + 1) * 128), gcol, None, AL.mult)

    em.cur_phase = 1
    def lnp_stats(src_buf, sc0, sset):
        base = sset * 20
        s6 = stats.v(0, 128, base, base + 12)
        s6a = stats.v(0, 128, base, base + 6)
        s6b = stats.v(0, 128, base + 6, base + 12)
        s2 = stats.v(0, 128, base + 12, base + 14)
        v_op("bn_stats", s6a, src_buf.v(0, 128, sc0, sc0 + 512))
        v_op("bn_stats", s6b, src_buf.v(0, 128, sc0 + 512, sc0 + 1024))
        v_op("bn_aggr", s2, s6)

    def lnp_rstd(sset):
        base = sset * 20
        var = stats.v(0, 128, base + 13, base + 14)
        rs = stats.v(0, 128, base + 14, base + 15)
        act(rs, var, AF.Ln, bias=(epsT.h[0:128, 0:1], epsT.v(0, 128, 0, 1)[1]))
        act(rs, rs, AF.Exp, scale=-0.5)

    def lnp_apply(src_buf, sc0, dst_v, sset):
        base = sset * 20
        mu = stats.v(0, 128, base + 12, base + 13)
        v_ts(dst_v, src_buf.v(0, 128, sc0, sc0 + 1024),
             (stats.h[0:128, base + 12:base + 13], mu[1]),
             (stats.h[0:128, base + 14:base + 15],
              stats.v(0, 128, base + 14, base + 15)[1]),
             AL.subtract, AL.mult)

    def ln_pass(src_buf, sc0, dst_v, sset, affine=None):
        lnp_stats(src_buf, sc0, sset)
        lnp_rstd(sset)
        lnp_apply(src_buf, sc0, dst_v, sset)

    def ln2_to_bf16(x_buf, xc0, out_v, sset):
        base = sset * 20
        s6 = stats.v(0, 128, base, base + 12)
        s6a = stats.v(0, 128, base, base + 6)
        s6b = stats.v(0, 128, base + 6, base + 12)
        s2 = stats.v(0, 128, base + 12, base + 14)
        var = stats.v(0, 128, base + 13, base + 14)
        mu = stats.v(0, 128, base + 12, base + 13)
        rs = stats.v(0, 128, base + 14, base + 15)
        v_op("bn_stats", s6a, x_buf.v(0, 128, xc0, xc0 + 512))
        v_op("bn_stats", s6b, x_buf.v(0, 128, xc0 + 512, xc0 + 1024))
        v_op("bn_aggr", s2, s6)
        act(rs, var, AF.Ln, scale=1.0 + EPS,
            bias=(epsT.h[0:128, 1:2], epsT.v(0, 128, 1, 2)[1]))
        act(rs, rs, AF.Exp, scale=-0.5)
        v_ts(out_v, x_buf.v(0, 128, xc0, xc0 + 1024),
             (stats.h[0:128, base + 12:base + 13], mu[1]),
             (stats.h[0:128, base + 14:base + 15], rs[1]),
             AL.subtract, AL.mult)

    _tile = [0]
    _pend = [None]

    def p1_tile(src_dbuf, srow, dst_dbuf, drow):
        ti = _tile[0]
        _tile[0] += 1
        t4 = ti % 2
        te = ti % 2
        dma("sync", scr.v(0, 128, t4 * 1024, (t4 + 1) * 1024),
            src_dbuf.v(srow, srow + 128, 0, D))
        base = te * 20
        s6a = stats.v(0, 128, base, base + 6)
        s6b = stats.v(0, 128, base + 6, base + 12)
        s2 = stats.v(0, 128, base + 12, base + 14)
        var = stats.v(0, 128, base + 13, base + 14)
        rs = stats.v(0, 128, base + 14, base + 15)
        v_op("bn_stats", s6a, scr.v(0, 128, t4 * 1024, t4 * 1024 + 512))
        v_op("bn_stats", s6b, scr.v(0, 128, t4 * 1024 + 512, t4 * 1024 + 1024))
        v_op("bn_aggr", s2, stats.v(0, 128, base, base + 12))
        act(rs, var, AF.Ln, scale=1.0 + EPS,
            bias=(epsT.h[0:128, 1:2], epsT.v(0, 128, 1, 2)[1]))
        act(rs, rs, AF.Exp, scale=-0.5)
        prev = _pend[0]

        def apply():
            out = PTb.v(0, 128, te * 1024, (te + 1) * 1024)
            v_ts(out, scr.v(0, 128, t4 * 1024, (t4 + 1) * 1024),
                 (stats.h[0:128, base + 12:base + 13],
                  stats.v(0, 128, base + 12, base + 13)[1]),
                 (stats.h[0:128, base + 14:base + 15],
                  stats.v(0, 128, base + 14, base + 15)[1]),
                 AL.subtract, AL.mult)
            dma("gpsimd", dst_dbuf.v(drow, drow + 128, 0, D), out)
        _pend[0] = apply
        if prev:
            prev()

    def p1_flush():
        if _pend[0]:
            _pend[0]()
            _pend[0] = None

    def p1_chunk(src_dbuf, src_row0, dst_dbuf, dst_row0):
        for t in range(4):
            p1_tile(src_dbuf, src_row0 + t * 128, dst_dbuf, dst_row0 + t * 128)

    def kv_pieces(ci):
        xt = xnTb
        ps = []
        for g in range(4):
            ps.append(lambda g=g: [dma(
                "scalar", xt.v(0, 128, ct * 1024, (ct + 1) * 1024),
                sp_xn.v(ci * 1024, (ci + 1) * 1024, ct * 128, (ct + 1) * 128),
                transpose=True) for ct in (2 * g, 2 * g + 1)])

        def mmset(h, which):
            for ct in range(8):
                if which == 0:
                    mm(PSv(0, 128, 3072, 3584),
                       wkvT.v(0, 128, ct * 128, (ct + 1) * 128),
                       xt.v(0, 128, ct * 1024 + h * 512, ct * 1024 + (h + 1) * 512),
                       start=(ct == 0), stop=(ct == 7))
                else:
                    mm(PSv(64, 128, 3584, 4096),
                       wkvT.v(0, 128, ct * 128, ct * 128 + 64),
                       xt.v(0, 128, ct * 1024 + h * 512, ct * 1024 + (h + 1) * 512),
                       start=(ct == 0), stop=(ct == 7), tp=(0, 64))

        def evac(h):
            c2 = ci * 2 + h
            bk0 = (bias2.h[0:64, 24:25], bias2.v(0, 64, 24, 25)[1])
            bk1 = (bias2.h[64:128, 25:26], bias2.v(64, 128, 25, 26)[1])
            kv0 = (ps_h[0:64, 3072:3584], PSv(0, 64, 3072, 3584)[1])
            kv1 = (ps_h[64:128, 3584:4096], PSv(64, 128, 3584, 4096)[1])
            em.add("vector", "c",
                   lambda e, o=kT2.h[0:64, c2 * 512:(c2 + 1) * 512], i=kv0[0],
                   s=bk0[0]: e.tensor_scalar(o, i, s, None, AL.add),
                   ins=[kv0[1], bk0[1]],
                   outs=[kT2.v(0, 64, c2 * 512, (c2 + 1) * 512)[1]])
            em.add("vector", "c",
                   lambda e, o=kT2.h[64:128, c2 * 512:(c2 + 1) * 512], i=kv1[0],
                   s=bk1[0]: e.tensor_scalar(o, i, s, None, AL.add),
                   ins=[kv1[1], bk1[1]],
                   outs=[kT2.v(64, 128, c2 * 512, (c2 + 1) * 512)[1]])

        def kmax(h):
            c2 = ci * 2 + h
            sq = xnTb.vr(xnTb.h[0:64, h * 512:h * 512 + 512],
                         0, 64, h * 512, h * 512 + 512)
            act(sq, kT2.v(0, 64, c2 * 512, (c2 + 1) * 512), AF.Square)
            mm(PSv(0, 1, 3584, 4096), ones64.v(0, 64, 0, 1), sq,
               start=True, stop=True)
            v_op("reduce_max", kmaxsq.v(0, 1, c2, c2 + 1), PSv(0, 1, 3584, 4096),
                 axis=AX.X)

        def vt(half):
            for tt in range(4 * half, 4 * half + 4):
                for ct in range(8):
                    mm(PSv(0, 128, 3072 + tt * 64, 3072 + (tt + 1) * 64),
                       xt.vr(xt.h[0:128, ct * 1024 + tt * 128:
                                  ct * 1024 + (tt + 1) * 128],
                             0, 128, ct * 1024 + tt * 128,
                             ct * 1024 + (tt + 1) * 128),
                       wkvT.v(0, 128, ct * 128 + 64, (ct + 1) * 128),
                       start=(ct == 0), stop=(ct == 7))

        def vaug_evac():
            bv3 = bias2.h[0:128, 32:96].unsqueeze(1).broadcast_to((128, 8, 64))
            va3 = vaug.h[0:128, ci * 512:(ci + 1) * 512].rearrange(
                "p (a b) -> p a b", a=8)
            pv3 = ps_h[0:128, 3072:3584].rearrange("p (a b) -> p a b", a=8)
            em.add("vector", "c",
                   lambda e, o=va3, i=pv3, s=bv3: e.tensor_tensor(o, i, s, AL.add),
                   ins=[PSv(0, 128, 3072, 3584)[1], bias2.v(0, 128, 32, 96)[1]],
                   outs=[vaug.v(0, 128, ci * 512, (ci + 1) * 512)[1]])

        for h in range(2):
            ps.append(lambda h=h: mmset(h, 0))
            ps.append(lambda h=h: mmset(h, 1))
            ps.append(lambda h=h: evac(h))
        ps.append(lambda: vt(0))
        ps.append(lambda: vt(1))
        ps.append(vaug_evac)
        ps.append(lambda: kmax(0))
        ps.append(lambda: kmax(1))
        return ps

    def p1kv_direct(ci):
        p1_chunk(x_full, ci * 1024, sp_xn, ci * 1024)
        p1_chunk(x_full, ci * 1024 + 512, sp_xn, ci * 1024 + 512)
        p1_flush()
        for p in kv_pieces(ci):
            p()

    if PHASES >= 2:
        for cj in range(4):
            p1_chunk(x_own, cj * 512, sp_xno, cj * 512)
        p1_flush()

    em.cur_phase = 2
    for cj in range(2 if PHASES >= 3 else 0):
        xt = xnTb
        for ct in range(8):
            dma("scalar", xt.v(0, 128, ct * 1024, (ct + 1) * 1024),
                sp_xno.v(cj * 1024, (cj + 1) * 1024, ct * 128, (ct + 1) * 128),
                transpose=True)
        for h in range(2):
            for et in range(8):
                cc = (et % 2) * 512 + h * 1024
                for ct in range(8):
                    mm(PSv(0, 128, cc, cc + 512),
                       wqT.v(0, 128, ct * 1024 + et * 128, ct * 1024 + (et + 1) * 128),
                       xt.v(0, 128, ct * 1024 + h * 512, ct * 1024 + (h + 1) * 512),
                       start=(ct == 0), stop=(ct == 7))
                bq = (bias2.h[0:128, 16 + et:17 + et],
                      bias2.v(0, 128, 16 + et, 17 + et)[1])
                v_ts(qT.v(0, 128, et * OWN + cj * 1024 + h * 512,
                          et * OWN + cj * 1024 + (h + 1) * 512),
                     PSv(0, 128, cc, cc + 512), bq, None, AL.add)

    def bound(sa):
        qsl_ap = qT.h[0:128, :].rearrange("p (a b) -> p a b", a=NPAIR)[
            :, :, sa * CH:(sa + 1) * CH]
        qacc = ("SB", 0, 128, qT.off, qT.off + NPAIR * OWN * 2)
        sq_ap = PTb.h[0:128, 0:2048].rearrange("p (a b) -> p a b", a=NPAIR)
        sqv = PTb.vr(sq_ap, 0, 128, 0, 2048)
        act(sqv, (qsl_ap, qacc), AF.Square)
        for qc in range(4):
            cc = 2048 + (qc % 2) * 512
            mm(PSv(0, 1, cc, cc + 512), ones64.v(0, 64, 0, 1),
               PTb.vr(PTb.h[0:64, qc * 512:(qc + 1) * 512], 0, 64, qc * 512, (qc + 1) * 512),
               start=True, stop=True)
            mm(PSv(32, 33, cc, cc + 512), ones64.v(64, 128, 0, 1),
               PTb.vr(PTb.h[64:128, qc * 512:(qc + 1) * 512], 64, 128, qc * 512, (qc + 1) * 512),
               start=True, stop=True, tp=(64, 32))
            v_op("reduce_max", smallA.v(0, 33, qc, qc + 1), PSv(0, 33, cc, cc + 512),
                 axis=AX.X)
        v_op("reduce_max", smallA.v(0, 33, 4, 5), smallA.v(0, 33, 0, 4), axis=AX.X)
        dma("sync", sp_misc.v(0, 1, 0, 1), smallA.v(0, 1, 4, 5))
        dma("sync", sp_misc.v(0, 1, 1, 2), smallA.v(32, 33, 4, 5))
        dma("sync", smallA.v(0, 1, 8, 10), sp_misc.v(0, 1, 0, 2))
        v_op("reduce_max", smallA.v(0, 1, 10, 11), smallA.v(0, 1, 8, 10), axis=AX.X)
        v_op("reduce_max", smallA.v(0, 1, 11, 12), kmaxsq.v(0, 1, 0, 2 * sa + 2),
             axis=AX.X)
        v_tt(smallA.v(0, 1, 12, 13), smallA.v(0, 1, 10, 11),
             smallA.v(0, 1, 11, 12), AL.mult)
        act(smallA.v(0, 1, 13, 14), smallA.v(0, 1, 12, 13), AF.Ln,
            bias=(epsT.h[0:1, 0:1], epsT.v(0, 1, 0, 1)[1]))
        act(smallA.v(0, 1, 13, 14), smallA.v(0, 1, 13, 14), AF.Exp, scale=0.5)
        v_ts(smallA.v(0, 1, 14, 15), smallA.v(0, 1, 13, 14), -1.0, SHIFT_MARGIN,
             AL.mult, AL.add)
        v_op("tensor_copy", bcolB.v(0, 1, 0, 1), smallA.v(0, 1, 14, 15))
        mm(PSv(0, 128, 3072, 3073), onesrow.v(0, 1, 0, 128), bcolB.v(0, 1, 0, 1),
           start=True, stop=True)
        v_op("tensor_copy", biasT.v(0, 128, sa, sa + 1), PSv(0, 128, 3072, 3073))

    em.cur_phase = 3
    import collections
    BG = collections.deque()
    _bgseq = [0]

    def bg_add(fn):
        _bgseq[0] += 1
        BG.append((_bgseq[0], fn))

    def bg_pump(n=1):
        for _ in range(n):
            if BG:
                BG.popleft()[1]()

    def bg_flush(seq=None):
        while BG and (seq is None or BG[0][0] <= seq):
            BG.popleft()[1]()

    def ffn_w_tile(src_in, sp_dst, ht, scale, seq):
        te = seq % 2
        wt_ap = xbuf.h.bitcast(I32)[0:128, 0:1024]
        wt_i = xbuf.vr(wt_ap, 0, 128, 0, 1024)
        c1 = min(src_in.cols, 1024)
        dma("sync", wt_i, src_in.v(ht * 128, (ht + 1) * 128, 0, c1))
        stg = PTb.v(0, 128, te * 1024, (te + 1) * 1024)
        v_ts(stg, wt_i, scale, None, AL.mult)
        dma("sync", sp_dst.v(ht * 128, (ht + 1) * 128, 0, c1), stg)

    def p3_slot(sa):
        nJ = 8 * (sa + 1)
        for g in range(P3_GROUPS):
            p0 = 2 * g
            pvb = 2048
            rsb = 2560

            def sim(J):
                simc = (J % 2) * 1024
                for pi in range(2):
                    pair = p0 + pi
                    for hf in range(2):
                        oc = simc + hf * 512 + pi * 256
                        mm(PSv(0, 128, oc, oc + 256),
                           kT2.v(64 * hf, 64 * hf + 64, J * 128, (J + 1) * 128),
                           qT.v(64 * hf, 64 * hf + 64,
                                pair * OWN + sa * CH, pair * OWN + (sa + 1) * CH),
                           start=(pi == 0), stop=(pi == 1),
                           tp=(64 * hf, 0))

            def expmask(J):
                pp = J % 2
                simc = pp * 1024
                ptv = etile.v(0, 128, pp * 1024, (pp + 1) * 1024)
                act(ptv, PSv(0, 128, simc, simc + 1024), AF.Exp,
                    bias=(biasT.h[0:128, sa:sa + 1], biasT.v(0, 128, sa, sa + 1)[1]))
                r = J - 8 * sa
                if r >= 0:
                    m_ap = maskS.h[0:128, r * CH:(r + 1) * CH].unsqueeze(1) \
                        .broadcast_to((128, 4, CH))
                    macc = maskS.v(0, 128, r * CH, (r + 1) * CH)[1]
                    pt4 = etile.h[0:128, pp * 1024:(pp + 1) * 1024] \
                        .rearrange("p (a b) -> p a b", a=4)
                    em.add("vector", "c",
                           lambda e, o=pt4, m=m_ap: e.tensor_tensor(o, o, m, AL.mult),
                           ins=[ptv[1], macc], outs=[ptv[1]])

            def pvrs(J):
                pp = J % 2
                for pi in range(2):
                    for hf in range(2):
                        mm(PSv(64 * hf, 64 * hf + 64,
                               pvb + pi * 256, pvb + (pi + 1) * 256),
                           vaug.v(0, 128, J * 64, (J + 1) * 64),
                           etile.v(0, 128, pp * 1024 + hf * 512 + pi * 256,
                                   pp * 1024 + hf * 512 + (pi + 1) * 256),
                           start=(J == 0 and pi == 0), stop=(J == nJ - 1 and pi == 1),
                           tp=(0, 64 * hf))
                for pi in range(2):
                    for hf in range(2):
                        mm(PSv(64 * hf, 64 * hf + 64,
                               rsb + pi * 256, rsb + (pi + 1) * 256),
                           ones64.v(0, 128, 0, 64),
                           etile.v(0, 128, pp * 1024 + hf * 512 + pi * 256,
                                   pp * 1024 + hf * 512 + (pi + 1) * 256),
                           start=(J == 0 and pi == 0), stop=(J == nJ - 1 and pi == 1),
                           tp=(0, 64 * hf))

            sim(0)
            if nJ > 1:
                sim(1)
            for J in range(nJ):
                expmask(J)
                if J >= 1:
                    pvrs(J - 1)
                if J + 2 < nJ:
                    sim(J + 2)
                bg_pump(1)
            pvrs(nJ - 1)
            v_op("reciprocal", scr.v(0, 128, 3072, 3584),
                 PSv(0, 128, rsb, rsb + 512))
            for pi in range(2):
                pair = p0 + pi
                v_tt(oT.v(0, 128, pair * OWN + sa * CH, pair * OWN + (sa + 1) * CH),
                     PSv(0, 128, pvb + pi * 256, pvb + (pi + 1) * 256),
                     scr.v(0, 128, 3072 + pi * 256, 3072 + (pi + 1) * 256),
                     AL.mult)

    for et in range(8 if PHASES >= 5 else 0):
        dma("scalar", woutT.v(0, 128, et * 1024, (et + 1) * 1024),
            sp_wout.v(0, D, et * 128, (et + 1) * 128), transpose=True)

    def p4_tile_bg(cj, tt):
        te = (cj * 4 + tt) % 2
        sset = 2 + te
        tok0 = cj * 512 + tt * 128
        a0 = 2048
        pieces = []
        pieces.append(lambda: dma(
            "gpsimd", xbuf.v(0, 128, 1024, 2048),
            x_own.v(tok0, tok0 + 128, 0, D)))

        def mmp(cc):
            pc = 3072 + cc * 512
            for et in range(8):
                mm(PSv(0, 128, pc, pc + 512),
                   oT.v(0, 128, et * OWN + tok0, et * OWN + tok0 + 128),
                   woutT.v(0, 128, et * 1024 + cc * 512,
                           et * 1024 + (cc + 1) * 512),
                   start=(et == 0), stop=(et == 7))
        pieces.append(lambda: mmp(0))
        pieces.append(lambda: mmp(1))

        def evac():
            for cc in range(2):
                v_op("tensor_copy",
                     scr.v(0, 128, a0 + cc * 512, a0 + (cc + 1) * 512),
                     PSv(0, 128, 3072 + cc * 512, 3072 + (cc + 1) * 512))
            lnp_stats(scr, a0, sset)
        pieces.append(evac)
        pieces.append(lambda: lnp_rstd(sset))

        def apply1():
            lnp_apply(scr, a0, scr.v(0, 128, a0, a0 + 1024), sset)
            v_tt(scr.v(0, 128, a0, a0 + 1024), scr.v(0, 128, a0, a0 + 1024),
                 lnoG.v(), AL.mult)
            v_tt(scr.v(0, 128, a0, a0 + 1024), scr.v(0, 128, a0, a0 + 1024),
                 lnoB.v(), AL.add)
            lnp_stats(scr, a0, sset)
        pieces.append(apply1)
        pieces.append(lambda: lnp_rstd(sset))

        def apply2():
            lnp_apply(scr, a0, scr.v(0, 128, a0, a0 + 1024), sset)
            v_tt(scr.v(0, 128, a0, a0 + 1024), scr.v(0, 128, a0, a0 + 1024),
                 xbuf.v(0, 128, 1024, 2048), AL.add)
            dma("sync", sp_x1.v(tok0, tok0 + 128, 0, D),
                scr.v(0, 128, a0, a0 + 1024))
            lnp_stats(scr, a0, sset)
        pieces.append(apply2)
        pieces.append(lambda: lnp_rstd(sset))

        def apply3():
            x1n = PTb.v(0, 128, te * 1024, (te + 1) * 1024)
            lnp_apply(scr, a0, x1n, sset)
            dma("sync", sp_x1n.v(tok0, tok0 + 128, 0, D), x1n)
        pieces.append(apply3)
        return pieces

    if PHASES >= 4:
        def wd_tile(seq):
            ct, part = seq // 3, seq % 3
            te = seq % 2
            wt_ap = xbuf.h.bitcast(I32)[0:128, 0:1024]
            wt_i = xbuf.vr(wt_ap, 0, 128, 0, 1024)
            dma("sync", wt_i,
                wd_in.v(ct * 128, (ct + 1) * 128, part * 1024, (part + 1) * 1024))
            stg = PTb.v(0, 128, te * 1024, (te + 1) * 1024)
            v_ts(stg, wt_i, DQ, None, AL.mult)
            dma("sync",
                sp_wd.v(ct * 128, (ct + 1) * 128, part * 1024, (part + 1) * 1024),
                stg)
    if PHASES >= 2:
        p1kv_direct(0)

    _p4_direct = []
    kv_seq = {}
    for sa in range(min(NSLOT, P3_SLOTS) if PHASES >= 4 else 0):
        em.cur_phase = 2
        if sa in kv_seq:
            bg_flush(kv_seq[sa])
        if PHASES >= 3:
            bound(sa)
        em.cur_phase = 3
        p3_slot(sa)
        em.cur_phase = 4
        ci = sa + 1
        if ci < NSLOT and PHASES >= 2:
            for t in range(8):
                bg_add(lambda ci=ci, t=t: p1_tile(
                    x_full, ci * 1024 + t * 128, sp_xn, ci * 1024 + t * 128))
            bg_add(p1_flush)
            for p in kv_pieces(ci):
                bg_add(p)
            kv_seq[ci] = _bgseq[0]
        if sa == 1 and PHASES >= 4:
            for ht in range(NHT):
                bg_add(lambda ht=ht: ffn_w_tile(wg_in, sp_wg, ht, DQ, ht))
        if sa == 2 and PHASES >= 4:
            for seq in range(24):
                bg_add(lambda seq=seq: wd_tile(seq))
        if sa == 6 and PHASES >= 4:
            for ct in range(8):
                bg_add(lambda ct=ct: dma(
                    "scalar", wgT.v(0, 128, ct * HID, (ct + 1) * HID),
                    sp_wg.v(0, HID, ct * 128, (ct + 1) * 128), transpose=True))
        if PHASES >= 5 and sa % 2 == 1:
            cj = sa // 2
            for tt in range(4):
                for p in p4_tile_bg(cj, tt):
                    (bg_add if P4_BG else _p4_direct.append)(p)
            for k in range(6):
                ht = cj * 6 + k
                if ht < NHT:
                    bg_add(lambda ht=ht: ffn_w_tile(wu_in, sp_wu, ht, DQ, ht))
    bg_flush()
    if "dbg_ot" in DBG:
        dma("sync", dbg_ot.v(0, 128, 0, NPAIR * OWN), oT.v())
    if "dbg_kt" in DBG:
        dma("sync", dbg_sa.v(), smallA.v(0, 128, 0, 32))
        dma("sync", dbg_bt.v(), biasT.v(0, 128, 0, 16))
        dma("sync", dbg_kt.v(), kT2.v())
        dma("sync", dbg_va.v(), vaug.v())
        dma("sync", dbg_km.v(), kmaxsq.v(0, 128, 0, 16))
    for p in _p4_direct:
        p()

    for ct in range(8 if PHASES >= 6 else 0):
        dma("scalar", wuT.v(0, 128, ct * HID, (ct + 1) * HID),
            sp_wu.v(0, HID, ct * 128, (ct + 1) * 128), transpose=True)
    for ht in range(NHT if PHASES >= 6 else 0):
        dma("scalar", wdT.v(0, 128, ht * 1024, (ht + 1) * 1024),
            sp_wd.v(0, D, ht * 128, (ht + 1) * 128), transpose=True)

    em.cur_phase = 5
    for ck in range(8 if PHASES >= 6 else 0):
        tok0 = ck * CH
        for ct in range(8):
            dma(tq(), PTb.v(0, 128, ct * CH, (ct + 1) * CH),
                sp_x1n.v(tok0, tok0 + CH, ct * 128, (ct + 1) * 128),
                transpose=True)
        for ht in range(NHT):
            for ct in range(8):
                mm(PSv(0, 128, 0, 256),
                   wgT.v(0, 128, ct * HID + ht * 128, ct * HID + (ht + 1) * 128),
                   PTb.v(0, 128, ct * CH, (ct + 1) * CH),
                   start=(ct == 0), stop=(ct == 7))
            for ct in range(8):
                mm(PSv(0, 128, 512, 768),
                   wuT.v(0, 128, ct * HID + ht * 128, ct * HID + (ht + 1) * 128),
                   PTb.v(0, 128, ct * CH, (ct + 1) * CH),
                   start=(ct == 0), stop=(ct == 7))
            act(scr.v(0, 128, 3072, 3072 + CH), PSv(0, 128, 0, 256), AF.Silu)
            v_tt(hsbH.v(0, 128, ht * CH, (ht + 1) * CH),
                 scr.v(0, 128, 3072, 3072 + CH), PSv(0, 128, 512, 768), AL.mult)
        for tt in range(2):
            for cc in range(2):
                pcol = 1024 + tt * 1024 + cc * 512
                for ht in range(NHT):
                    mm(PSv(0, 128, pcol, pcol + 512),
                       hsbH.v(0, 128, ht * CH + tt * 128, ht * CH + tt * 128 + 128),
                       wdT.v(0, 128, ht * 1024 + cc * 512, ht * 1024 + (cc + 1) * 512),
                       start=(ht == 0), stop=(ht == NHT - 1))
        for tt in range(2):
            te = tt % 2
            pcol0 = 1024 + tt * 1024
            v_op("tensor_copy", scr.v(0, 128, te * 2048, te * 2048 + 512),
                 PSv(0, 128, pcol0, pcol0 + 512))
            v_op("tensor_copy", scr.v(0, 128, te * 2048 + 512, te * 2048 + 1024),
                 PSv(0, 128, pcol0 + 512, pcol0 + 1024))
            a0 = te * 2048
            o0 = te * 2048 + 1024
            ln_pass(scr, a0, scr.v(0, 128, o0, o0 + 1024), te)
            dma("gpsimd", xbuf.v(0, 128, te * 1024, (te + 1) * 1024),
                sp_x1.v(tok0 + tt * 128, tok0 + (tt + 1) * 128, 0, D))
            v_tt(scr.v(0, 128, o0, o0 + 1024), scr.v(0, 128, o0, o0 + 1024),
                 xbuf.v(0, 128, te * 1024, (te + 1) * 1024), AL.add)
            dma("sync", y_own.v(tok0 + tt * 128, tok0 + (tt + 1) * 128, 0, D),
                scr.v(0, 128, o0, o0 + 1024))

    if PHASES < 6:
        for t in range(OWN // 128):
            memset(scr.v(0, 128, 0, 1024), 0.0)
            dma("sync", y_own.v(t * 128, (t + 1) * 128, 0, D), scr.v(0, 128, 0, 1024))

    counters = em.resolve()
    sem_keys = sorted({(op.kind, op.eng) for op in em.ops if op.inc})
    by_eng = {}
    for op in em.ops:
        by_eng.setdefault(op.eng, []).append(op)

    sems = {}
    es = contextlib.ExitStack()
    with es:
        for key in sem_keys:
            sems[key] = es.enter_context(nc.semaphore(f"sem_{key[0]}_{key[1]}"))
        block = es.enter_context(nc.Block())

        def make_stream(ops, is_sync):
            def body(e):
                for op in ops:
                    for (key, val) in op.waits:
                        e.wait_ge(sems[key], val)
                    inst = op.emit(e)
                    if op.inc:
                        inst.then_inc(sems[(op.kind, op.eng)],
                                      16 if op.kind == "d" else 1)
                if is_sync:
                    for k in sorted(counters):
                        if k[0] == "d":
                            e.wait_ge(sems[k], counters[k])
            return body

        for eng_name in ("tensor", "vector", "scalar", "gpsimd", "sync"):
            ops = by_eng.get(eng_name, [])
            getattr(block, eng_name)(make_stream(ops, eng_name == "sync"))
    psum_ctx.close()
    return nc


_prog_cache = {}


def _get_program():
    if "nc" not in _prog_cache:
        _prog_cache["nc"] = build_program()
    return _prog_cache["nc"]


def _build_mask(cp):
    m = np.zeros((8, 128, CH), dtype=np.float32)
    j = np.arange(128)[:, None]
    q = np.arange(CH)[None, :]
    for r in range(8):
        if r < 2 * cp:
            m[r] = 1.0
        elif r < 2 * cp + 2:
            p = r - 2 * cp
            m[r] = (128 * p + j <= q).astype(np.float32)
    return np.ascontiguousarray(m.reshape(8 * 128, CH).astype(ml_dtypes.bfloat16))


def kernel(L0_attn_x, L0_attn_w_q, L0_attn_w_kv, L0_attn_w_out,
           L0_attn_ln_g, L0_attn_ln_b, L0_attn_lno_g, L0_attn_lno_b,
           L0_ffn_w_gate, L0_ffn_w_up, L0_ffn_w_down):
    x = np.ascontiguousarray(np.asarray(L0_attn_x, dtype=np.float32))
    wq = np.ascontiguousarray(np.asarray(L0_attn_w_q, dtype=np.float32))
    wkv = np.ascontiguousarray(np.asarray(L0_attn_w_kv, dtype=np.float32))
    wout = np.ascontiguousarray(np.asarray(L0_attn_w_out, dtype=np.float32))
    lngT = np.ascontiguousarray(
        np.asarray(L0_attn_ln_g, dtype=np.float32).reshape(8, 128).T)
    lnbT = np.ascontiguousarray(
        np.asarray(L0_attn_ln_b, dtype=np.float32).reshape(8, 128).T)
    lnog = np.ascontiguousarray(np.broadcast_to(
        np.asarray(L0_attn_lno_g, dtype=np.float32)[None, :], (128, D)))
    lnob = np.ascontiguousarray(np.broadcast_to(
        np.asarray(L0_attn_lno_b, dtype=np.float32)[None, :], (128, D)))
    wg = np.ascontiguousarray(np.asarray(L0_ffn_w_gate, dtype=np.int32))
    wu = np.ascontiguousarray(np.asarray(L0_ffn_w_up, dtype=np.int32))
    wd = np.ascontiguousarray(np.asarray(L0_ffn_w_down, dtype=np.int32))

    nc = _get_program()
    in_maps = []
    for core in range(8):
        b = core // 4
        cp = core % 4
        ids = [4 * a + cp for a in range(NSLOT)]
        xb = np.ascontiguousarray(x[b])
        x_own_np = np.ascontiguousarray(
            np.concatenate([xb[k * CH:(k + 1) * CH] for k in ids], axis=0))
        in_maps.append({
            "x_full": xb,
            "x_own": x_own_np,
            "wq": wq, "wkv": wkv, "wout": wout,
            "lngT": lngT, "lnbT": lnbT, "lnog": lnog, "lnob": lnob,
            "wg": wg, "wu": wu, "wd": wd,
            "maskt": _build_mask(cp),
        })
    res = run_bass_kernel_spmd(nc, in_maps, core_ids=list(range(8)))
    global _last_res
    _last_res = res
    out = np.empty((B, N, D), dtype=np.float32)
    for core in range(8):
        b = core // 4
        cp = core % 4
        yo = np.asarray(res.results[core]["y_own"])
        for a in range(NSLOT):
            k = 4 * a + cp
            out[b, k * CH:(k + 1) * CH] = yo[a * CH:(a + 1) * CH]
    return out


# version 11
# speedup vs baseline: 1.2237x; 1.2237x over previous
import contextlib
import numpy as np
import ml_dtypes
import concourse.bass as bass
import concourse.mybir as mybir
from concourse.bass_utils import run_bass_kernel_spmd

F32 = mybir.dt.float32
BF16 = mybir.dt.bfloat16
I32 = mybir.dt.int32
AF = mybir.ActivationFunctionType
AL = mybir.AluOpType
AX = mybir.AxisListType

B = 2
N = 8192
D = 1024
H = 16
DH = 64
HID = 3072
NPAIR = H // 2
CH = 256
NSLOT = 8
OWN = NSLOT * CH
NCT = D // 128
NHT = HID // 128
EPS = 1e-5
QSCALE = 1.0 / 64.0
DQ = 1.0 / 127.0
SHIFT_MARGIN = 30.0

ESZ = {F32: 4, BF16: 2, I32: 4}

PARANOID = False
PARANOID_PHASES = None
PHASES = 99
DBG = set()
P3_SLOTS = 99
P3_GROUPS = 4
P3_PARTS = {"sim", "exp", "mask", "pv", "rs", "norm"}


class _Op:
    __slots__ = ("idx", "eng", "kind", "emit", "deps", "inc", "count", "waits",
                 "phase")

    def __init__(self, idx, eng, kind, emit, deps):
        self.phase = 0
        self.idx = idx
        self.eng = eng
        self.kind = kind
        self.emit = emit
        self.deps = deps
        self.inc = kind == "d"
        self.count = None
        self.waits = []


class Em:
    def __init__(self):
        self.ops = []
        self.wstate = {}
        self.rstate = {}
        self.cur_phase = 0

    @staticmethod
    def _overlap(a, b):
        return a[1] > b[0] and b[1] > a[0] and a[3] > b[2] and b[3] > a[2]

    def _scan(self, state, acc):
        lst = state.get(acc[0])
        if not lst:
            return ()
        r = acc[1:]
        return tuple(e[4] for e in lst if self._overlap(r, e))

    def add(self, eng, kind, emit, ins=(), outs=()):
        idx = len(self.ops)
        deps = set()
        for acc in ins:
            deps.update(self._scan(self.wstate, acc))
        for acc in outs:
            deps.update(self._scan(self.wstate, acc))
            deps.update(self._scan(self.rstate, acc))
        deps.discard(idx)
        for acc in ins:
            self.rstate.setdefault(acc[0], []).append(
                (acc[1], acc[2], acc[3], acc[4], idx))
        for acc in outs:
            key, p0, p1, b0, b1 = acc
            for state in (self.wstate, self.rstate):
                lst = state.get(key)
                if lst:
                    lst[:] = [e for e in lst
                              if not (p0 <= e[0] and e[1] <= p1
                                      and b0 <= e[2] and e[3] <= b1)]
            self.wstate.setdefault(key, []).append((p0, p1, b0, b1, idx))
        op = _Op(idx, eng, kind, emit, deps)
        op.phase = self.cur_phase
        self.ops.append(op)
        return op

    def resolve(self):
        ops = self.ops
        for op in ops:
            for d in op.deps:
                dop = ops[d]
                if dop.eng == "tensor" and op.eng == "tensor" \
                        and dop.kind == "c" and op.kind == "c":
                    continue
                dop.inc = True
        counters = {}
        for op in ops:
            if op.inc:
                key = (op.kind, op.eng)
                counters[key] = counters.get(key, 0) + (16 if op.kind == "d" else 1)
                op.count = counters[key]
        stream_guar = {}
        last_count = {}
        for op in ops:
            base = stream_guar.setdefault(op.eng, {})
            needed = {}
            if PARANOID and (PARANOID_PHASES is None
                             or op.phase in PARANOID_PHASES):
                for key, val in last_count.items():
                    if key[1] != op.eng or key[0] != op.kind:
                        needed[key] = val
            for d in op.deps:
                dop = ops[d]
                if dop.eng == "tensor" and op.eng == "tensor" \
                        and dop.kind == "c" and op.kind == "c":
                    continue
                key = (dop.kind, dop.eng)
                if needed.get(key, -1) < dop.count:
                    needed[key] = dop.count
            if op.inc:
                last_count[(op.kind, op.eng)] = op.count
            waits = []
            for key, val in sorted(needed.items()):
                if base.get(key, -1) < val:
                    waits.append((key, val))
                    base[key] = val
            op.waits = waits
        return counters


SB_BASE = 24576


class SBuf:
    def __init__(self, nc, name, cols, dtype, off):
        off = off + SB_BASE
        self.h = nc.alloc_sbuf_tensor_at(name, [128, cols], dtype, offset=off)
        self.esz = ESZ[dtype]
        self.off = off
        self.cols = cols

    def v(self, p0=0, p1=128, c0=0, c1=None):
        if c1 is None:
            c1 = self.cols
        ap = self.h[p0:p1, c0:c1]
        acc = ("SB", p0, p1, self.off + c0 * self.esz, self.off + c1 * self.esz)
        return ap, acc

    def vr(self, ap, p0, p1, c0, c1):
        return ap, ("SB", p0, p1, self.off + c0 * self.esz, self.off + c1 * self.esz)


class DBuf:
    def __init__(self, nc, name, rows, cols, dtype, kind="Internal"):
        if kind == "Internal" and name in DBG:
            kind = "ExternalOutput"
        self.h = nc.dram_tensor(name, [rows, cols], dtype, kind=kind)
        self.esz = ESZ[dtype]
        self.name = name
        self.rows = rows
        self.cols = cols

    def v(self, r0=0, r1=None, c0=0, c1=None):
        if r1 is None:
            r1 = self.rows
        if c1 is None:
            c1 = self.cols
        ap = self.h.ap()[r0:r1, c0:c1]
        acc = ("DR:" + self.name, r0, r1, c0 * self.esz, c1 * self.esz)
        return ap, acc


def build_program():
    nc = bass.Bass("TRN2", target_bir_lowering=False)
    em = Em()

    x_full = DBuf(nc, "x_full", N, D, F32, kind="ExternalInput")
    x_own = DBuf(nc, "x_own", OWN, D, F32, kind="ExternalInput")
    wq_in = DBuf(nc, "wq", D, D, F32, kind="ExternalInput")
    wkv_in = DBuf(nc, "wkv", 2 * DH, D, F32, kind="ExternalInput")
    lngT_in = DBuf(nc, "lngT", 128, 8, F32, kind="ExternalInput")
    lnbT_in = DBuf(nc, "lnbT", 128, 8, F32, kind="ExternalInput")
    wout_in = DBuf(nc, "wout", D, D, F32, kind="ExternalInput")
    lnog_in = DBuf(nc, "lnog", 128, D, F32, kind="ExternalInput")
    lnob_in = DBuf(nc, "lnob", 128, D, F32, kind="ExternalInput")
    wg_in = DBuf(nc, "wg", HID, D, I32, kind="ExternalInput")
    wu_in = DBuf(nc, "wu", HID, D, I32, kind="ExternalInput")
    wd_in = DBuf(nc, "wd", D, HID, I32, kind="ExternalInput")
    mask_in = DBuf(nc, "maskt", 8 * 128, CH, BF16, kind="ExternalInput")
    y_own = DBuf(nc, "y_own", OWN, D, F32, kind="ExternalOutput")

    sp_xn = DBuf(nc, "sp_xn", N, D, BF16)
    sp_xno = DBuf(nc, "sp_xno", OWN, D, BF16)
    sp_wq = DBuf(nc, "sp_wq", D, D, BF16)
    sp_wkv = DBuf(nc, "sp_wkv", 2 * DH, D, BF16)
    sp_wout = DBuf(nc, "sp_wout", D, D, BF16)
    sp_wg = DBuf(nc, "sp_wg", HID, D, BF16)
    sp_wu = DBuf(nc, "sp_wu", HID, D, BF16)
    sp_wd = DBuf(nc, "sp_wd", D, HID, BF16)
    sp_x1 = DBuf(nc, "sp_x1", OWN, D, F32)
    sp_x1n = DBuf(nc, "sp_x1n", OWN, D, BF16)
    sp_misc = DBuf(nc, "sp_misc", 8, 8, F32)

    maskS = SBuf(nc, "maskS", 8 * CH, BF16, 0)
    biasT = SBuf(nc, "biasT", 16, F32, 4096)
    ones64 = SBuf(nc, "ones64", 128, BF16, 4160)
    onesrow = SBuf(nc, "onesrow", 128, BF16, 4416)
    epsT = SBuf(nc, "epsT", 8, F32, 4672)
    kmaxsq = SBuf(nc, "kmaxsq", 16, F32, 4704)
    smallA = SBuf(nc, "smallA", 32, F32, 4768)
    bcolB = SBuf(nc, "bcolB", 8, BF16, 4896)
    stats = SBuf(nc, "stats", 40, F32, 4928)
    bias2 = SBuf(nc, "bias2", 96, F32, 5088)
    lnbTb = SBuf(nc, "lnbTb", 8, BF16, 5472)
    PTb = SBuf(nc, "PTb", 2048, BF16, 5536)
    xbuf = SBuf(nc, "xbuf", 2048, F32, 9632)
    scr = SBuf(nc, "scr", 4096, F32, 17824)
    etile = SBuf(nc, "etile", 2048, BF16, 34208)
    lnoG = SBuf(nc, "lnoG", D, F32, 39328)
    lnoB = SBuf(nc, "lnoB", D, F32, 43424)
    vaug = SBuf(nc, "vaug", 4096, BF16, 47520)
    wqT = SBuf(nc, "wqT", 8 * D, BF16, 55712)
    wkvT = SBuf(nc, "wkvT", 8 * 128, BF16, 72096)
    xnTb = SBuf(nc, "xnTb", 8 * 1024, BF16, 74144)
    wgT = SBuf(nc, "wgT", 8 * HID, BF16, 55712)
    qT = SBuf(nc, "qT", NPAIR * OWN, BF16, 104864)
    kT2 = SBuf(nc, "kT2", N, BF16, 137632)
    wuT = SBuf(nc, "wuT", 8 * HID, BF16, 104864)
    oT = SBuf(nc, "oT", NPAIR * OWN, BF16, 154016)
    woutT = SBuf(nc, "woutT", 8 * D, BF16, 186784)
    wdT = SBuf(nc, "wdT", NHT * D, BF16, 154016)
    hsbH = SBuf(nc, "hsbH", NHT * CH, BF16, 34208)

    psum_ctx = contextlib.ExitStack()
    ps_h = psum_ctx.enter_context(nc.psum_tensor("ps", [128, 4096], F32))

    def PSv(p0, p1, c0, c1):
        ap = ps_h[p0:p1, c0:c1]
        b0 = (c0 // 512) * 2048
        b1 = ((c1 + 511) // 512) * 2048
        return ap, ("PS", 0, 128, b0, b1)

    def mm(out_v, lhs_v, rhs_v, start, stop, tp=None):
        (oap, oacc) = out_v
        (lap, lacc) = lhs_v
        (rap, racc) = rhs_v
        kw = {}
        if tp is not None:
            kw["tile_position"] = tp
        em.add("tensor", "c",
               lambda e, oap=oap, lap=lap, rap=rap, kw=kw, start=start, stop=stop:
               e.matmul(oap, lap, rap, start=start, stop=stop, **kw),
               ins=[lacc, racc], outs=[oacc])

    def act(out_v, in_v, func, bias=0.0, scale=1.0):
        (oap, oacc) = out_v
        (iap, iacc) = in_v
        ins = [iacc]
        bap = bias
        if isinstance(bias, tuple):
            bap = bias[0]
            ins.append(bias[1])
        em.add("scalar", "c",
               lambda e, oap=oap, iap=iap, func=func, bap=bap, scale=scale:
               e.activation(oap, iap, func, bias=bap, scale=scale),
               ins=ins, outs=[oacc])

    def v_op(fn_name, out_v, *in_vs, **kw):
        (oap, oacc) = out_v
        iaps = [iv[0] for iv in in_vs]
        iaccs = [iv[1] for iv in in_vs]

        def emit(e, fn_name=fn_name, oap=oap, iaps=iaps, kw=kw):
            return getattr(e, fn_name)(oap, *iaps, **kw)
        em.add("vector", "c", emit, ins=iaccs, outs=[oacc])

    def v_ts(out_v, in_v, s1, s2, op0, op1=None):
        (oap, oacc) = out_v
        (iap, iacc) = in_v
        ins = [iacc]
        s1ap, s2ap = s1, s2
        if isinstance(s1, tuple):
            s1ap = s1[0]
            ins.append(s1[1])
        if isinstance(s2, tuple):
            s2ap = s2[0]
            ins.append(s2[1])
        kw = {}
        if op1 is not None:
            kw["op1"] = op1
        em.add("vector", "c",
               lambda e, oap=oap, iap=iap, s1ap=s1ap, s2ap=s2ap, op0=op0, kw=kw:
               e.tensor_scalar(oap, iap, s1ap, s2ap, op0, **kw),
               ins=ins, outs=[oacc])

    def v_tt(out_v, a_v, b_v, op):
        v_op("tensor_tensor", out_v, a_v, b_v, op=op)

    def dma(queue, out_v, in_v, transpose=False):
        (oap, oacc) = out_v
        (iap, iacc) = in_v
        if transpose:
            em.add(queue, "d",
                   lambda e, oap=oap, iap=iap: e.dma_start_transpose(oap, iap),
                   ins=[iacc], outs=[oacc])
        else:
            em.add(queue, "d",
                   lambda e, oap=oap, iap=iap: e.dma_start(out=oap, in_=iap),
                   ins=[iacc], outs=[oacc])

    _tq = [0]

    def tq():
        _tq[0] ^= 1
        return "scalar"

    def memset(out_v, val):
        (oap, oacc) = out_v
        em.add("vector", "c", lambda e, oap=oap, val=val: e.memset(oap, val),
               ins=[], outs=[oacc])

    memset(epsT.v(0, 128, 0, 1), EPS)
    memset(epsT.v(0, 128, 1, 2), EPS * EPS)
    memset(ones64.v(0, 128, 0, 64), 1.0)
    memset(ones64.v(0, 128, 64, 128), 1.0)
    memset(onesrow.v(0, 1, 0, 128), 1.0)

    for r in range(8):
        dma("gpsimd", maskS.v(0, 128, r * CH, (r + 1) * CH),
            mask_in.v(r * 128, (r + 1) * 128, 0, CH))
    dma("gpsimd", bias2.v(0, 128, 0, 8), lngT_in.v())
    dma("gpsimd", bias2.v(0, 128, 8, 16), lnbT_in.v())
    dma("gpsimd", lnoG.v(), lnog_in.v())
    dma("gpsimd", lnoB.v(), lnob_in.v())

    for et in range(8):
        te = et % 2
        wt_f = scr.v(0, 128, te * 1024, (te + 1) * 1024)
        dma("sync", wt_f, wq_in.v(et * 128, (et + 1) * 128, 0, D))
        stg = PTb.v(0, 128, te * 1024, (te + 1) * 1024)
        v_ts(stg, wt_f, QSCALE, None, AL.mult)
        dma("sync", sp_wq.v(et * 128, (et + 1) * 128, 0, D), stg)
    for ct in range(8):
        dma("scalar", wqT.v(0, 128, ct * 1024, (ct + 1) * 1024),
            sp_wq.v(0, D, ct * 128, (ct + 1) * 128), transpose=True)

    dma("gpsimd", sp_wkv.v(), wkv_in.v())
    for ct in range(8):
        dma("scalar", wkvT.v(0, 128, ct * 128, (ct + 1) * 128),
            sp_wkv.v(0, 2 * DH, ct * 128, (ct + 1) * 128), transpose=True)

    dma("gpsimd", sp_wout.v(), wout_in.v())

    v_op("tensor_copy", lnbTb.v(0, 128, 0, 8), bias2.v(0, 128, 8, 16))
    for et in range(8):
        for ct in range(8):
            mm(PSv(0, 128, 3072 + et, 3073 + et),
               wqT.v(0, 128, ct * 1024 + et * 128, ct * 1024 + (et + 1) * 128),
               lnbTb.vr(lnbTb.h[0:128, ct:ct + 1], 0, 128, ct, ct + 1),
               start=(ct == 0), stop=(ct == 7))
    for ct in range(8):
        mm(PSv(0, 128, 3080, 3081),
           wkvT.v(0, 128, ct * 128, (ct + 1) * 128),
           lnbTb.vr(lnbTb.h[0:128, ct:ct + 1], 0, 128, ct, ct + 1),
           start=(ct == 0), stop=(ct == 7))
        mm(PSv(64, 128, 3081, 3082),
           wkvT.v(0, 128, ct * 128, ct * 128 + 64),
           lnbTb.vr(lnbTb.h[0:128, ct:ct + 1], 0, 128, ct, ct + 1),
           start=(ct == 0), stop=(ct == 7), tp=(0, 64))
    v_op("tensor_copy", bias2.v(0, 128, 16, 26), PSv(0, 128, 3072, 3082))
    for ct in range(8):
        mm(PSv(0, 1, 3584, 3712),
           lnbTb.vr(lnbTb.h[0:128, ct:ct + 1], 0, 128, ct, ct + 1),
           wkvT.v(0, 128, ct * 128, (ct + 1) * 128),
           start=(ct == 0), stop=(ct == 7))
    v_op("tensor_copy", etile.v(0, 1, 0, 128), PSv(0, 1, 3584, 3712))
    mm(PSv(0, 128, 3712, 3776), onesrow.v(0, 1, 0, 128), etile.v(0, 1, 64, 128),
       start=True, stop=True)
    v_op("tensor_copy", bias2.v(0, 128, 32, 96), PSv(0, 128, 3712, 3776))
    for ct in range(8):
        gcol = (bias2.h[0:128, ct:ct + 1], bias2.v(0, 128, ct, ct + 1)[1])
        v_ts(wqT.v(0, 128, ct * 1024, (ct + 1) * 1024),
             wqT.v(0, 128, ct * 1024, (ct + 1) * 1024), gcol, None, AL.mult)
        v_ts(wkvT.v(0, 128, ct * 128, (ct + 1) * 128),
             wkvT.v(0, 128, ct * 128, (ct + 1) * 128), gcol, None, AL.mult)

    em.cur_phase = 1
    def lnp_stats(src_buf, sc0, sset):
        base = sset * 20
        s6 = stats.v(0, 128, base, base + 12)
        s6a = stats.v(0, 128, base, base + 6)
        s6b = stats.v(0, 128, base + 6, base + 12)
        s2 = stats.v(0, 128, base + 12, base + 14)
        v_op("bn_stats", s6a, src_buf.v(0, 128, sc0, sc0 + 512))
        v_op("bn_stats", s6b, src_buf.v(0, 128, sc0 + 512, sc0 + 1024))
        v_op("bn_aggr", s2, s6)

    def lnp_rstd(sset):
        base = sset * 20
        var = stats.v(0, 128, base + 13, base + 14)
        rs = stats.v(0, 128, base + 14, base + 15)
        act(rs, var, AF.Ln, bias=(epsT.h[0:128, 0:1], epsT.v(0, 128, 0, 1)[1]))
        act(rs, rs, AF.Exp, scale=-0.5)

    def lnp_apply(src_buf, sc0, dst_v, sset):
        base = sset * 20
        mu = stats.v(0, 128, base + 12, base + 13)
        v_ts(dst_v, src_buf.v(0, 128, sc0, sc0 + 1024),
             (stats.h[0:128, base + 12:base + 13], mu[1]),
             (stats.h[0:128, base + 14:base + 15],
              stats.v(0, 128, base + 14, base + 15)[1]),
             AL.subtract, AL.mult)

    def ln_pass(src_buf, sc0, dst_v, sset, affine=None):
        lnp_stats(src_buf, sc0, sset)
        lnp_rstd(sset)
        lnp_apply(src_buf, sc0, dst_v, sset)

    def ln2_to_bf16(x_buf, xc0, out_v, sset):
        base = sset * 20
        s6 = stats.v(0, 128, base, base + 12)
        s6a = stats.v(0, 128, base, base + 6)
        s6b = stats.v(0, 128, base + 6, base + 12)
        s2 = stats.v(0, 128, base + 12, base + 14)
        var = stats.v(0, 128, base + 13, base + 14)
        mu = stats.v(0, 128, base + 12, base + 13)
        rs = stats.v(0, 128, base + 14, base + 15)
        v_op("bn_stats", s6a, x_buf.v(0, 128, xc0, xc0 + 512))
        v_op("bn_stats", s6b, x_buf.v(0, 128, xc0 + 512, xc0 + 1024))
        v_op("bn_aggr", s2, s6)
        act(rs, var, AF.Ln, scale=1.0 + EPS,
            bias=(epsT.h[0:128, 1:2], epsT.v(0, 128, 1, 2)[1]))
        act(rs, rs, AF.Exp, scale=-0.5)
        v_ts(out_v, x_buf.v(0, 128, xc0, xc0 + 1024),
             (stats.h[0:128, base + 12:base + 13], mu[1]),
             (stats.h[0:128, base + 14:base + 15], rs[1]),
             AL.subtract, AL.mult)

    _tile = [0]
    _pend = [None]

    def p1_tile(src_dbuf, srow, dst_dbuf, drow):
        ti = _tile[0]
        _tile[0] += 1
        t4 = ti % 4
        te = ti % 2
        dma("sync", scr.v(0, 128, t4 * 1024, (t4 + 1) * 1024),
            src_dbuf.v(srow, srow + 128, 0, D))
        base = te * 20
        s6a = stats.v(0, 128, base, base + 6)
        s6b = stats.v(0, 128, base + 6, base + 12)
        s2 = stats.v(0, 128, base + 12, base + 14)
        var = stats.v(0, 128, base + 13, base + 14)
        rs = stats.v(0, 128, base + 14, base + 15)
        v_op("bn_stats", s6a, scr.v(0, 128, t4 * 1024, t4 * 1024 + 512))
        v_op("bn_stats", s6b, scr.v(0, 128, t4 * 1024 + 512, t4 * 1024 + 1024))
        v_op("bn_aggr", s2, stats.v(0, 128, base, base + 12))
        act(rs, var, AF.Ln, scale=1.0 + EPS,
            bias=(epsT.h[0:128, 1:2], epsT.v(0, 128, 1, 2)[1]))
        act(rs, rs, AF.Exp, scale=-0.5)
        prev = _pend[0]

        def apply():
            out = PTb.v(0, 128, te * 1024, (te + 1) * 1024)
            v_ts(out, scr.v(0, 128, t4 * 1024, (t4 + 1) * 1024),
                 (stats.h[0:128, base + 12:base + 13],
                  stats.v(0, 128, base + 12, base + 13)[1]),
                 (stats.h[0:128, base + 14:base + 15],
                  stats.v(0, 128, base + 14, base + 15)[1]),
                 AL.subtract, AL.mult)
            dma("gpsimd", dst_dbuf.v(drow, drow + 128, 0, D), out)
        _pend[0] = apply
        if prev:
            prev()

    def p1_flush():
        if _pend[0]:
            _pend[0]()
            _pend[0] = None

    def p1_chunk(src_dbuf, src_row0, dst_dbuf, dst_row0):
        for t in range(4):
            p1_tile(src_dbuf, src_row0 + t * 128, dst_dbuf, dst_row0 + t * 128)

    def kv_chunk(ci):
        xt = xnTb
        for ct in range(8):
            dma("scalar", xt.v(0, 128, ct * 1024, (ct + 1) * 1024),
                sp_xn.v(ci * 1024, (ci + 1) * 1024, ct * 128, (ct + 1) * 128),
                transpose=True)
        for h in range(2):
            cA = h * 512
            cB = 1024 + h * 512
            for ct in range(8):
                mm(PSv(0, 128, cA, cA + 512),
                   wkvT.v(0, 128, ct * 128, (ct + 1) * 128),
                   xt.v(0, 128, ct * 1024 + h * 512, ct * 1024 + (h + 1) * 512),
                   start=(ct == 0), stop=(ct == 7))
            for ct in range(8):
                mm(PSv(64, 128, cB, cB + 512),
                   wkvT.v(0, 128, ct * 128, ct * 128 + 64),
                   xt.v(0, 128, ct * 1024 + h * 512, ct * 1024 + (h + 1) * 512),
                   start=(ct == 0), stop=(ct == 7), tp=(0, 64))
        bk0 = (bias2.h[0:64, 24:25], bias2.v(0, 64, 24, 25)[1])
        bk1 = (bias2.h[64:128, 25:26], bias2.v(64, 128, 25, 26)[1])
        kv0 = (ps_h[0:64, 0:1024], PSv(0, 64, 0, 1024)[1])
        kv1 = (ps_h[64:128, 1024:2048], PSv(64, 128, 1024, 2048)[1])
        em.add("vector", "c",
               lambda e, o=kT2.h[0:64, ci * 1024:(ci + 1) * 1024], i=kv0[0], s=bk0[0]:
               e.tensor_scalar(o, i, s, None, AL.add),
               ins=[kv0[1], bk0[1]],
               outs=[kT2.v(0, 64, ci * 1024, (ci + 1) * 1024)[1]])
        em.add("vector", "c",
               lambda e, o=kT2.h[64:128, ci * 1024:(ci + 1) * 1024], i=kv1[0], s=bk1[0]:
               e.tensor_scalar(o, i, s, None, AL.add),
               ins=[kv1[1], bk1[1]],
               outs=[kT2.v(64, 128, ci * 1024, (ci + 1) * 1024)[1]])
        for tt in range(8):
            for ct in range(8):
                mm(PSv(0, 128, 2048 + tt * 64, 2048 + (tt + 1) * 64),
                   xt.vr(xt.h[0:128, ct * 1024 + tt * 128: ct * 1024 + (tt + 1) * 128],
                         0, 128, ct * 1024 + tt * 128, ct * 1024 + (tt + 1) * 128),
                   wkvT.v(0, 128, ct * 128 + 64, (ct + 1) * 128),
                   start=(ct == 0), stop=(ct == 7))
        bv3 = bias2.h[0:128, 32:96].unsqueeze(1).broadcast_to((128, 8, 64))
        va3 = vaug.h[0:128, ci * 512:(ci + 1) * 512].rearrange(
            "p (a b) -> p a b", a=8)
        pv3 = ps_h[0:128, 2048:2560].rearrange("p (a b) -> p a b", a=8)
        em.add("vector", "c",
               lambda e, o=va3, i=pv3, s=bv3: e.tensor_tensor(o, i, s, AL.add),
               ins=[PSv(0, 128, 2048, 2560)[1], bias2.v(0, 128, 32, 96)[1]],
               outs=[vaug.v(0, 128, ci * 512, (ci + 1) * 512)[1]])
        for h in range(2):
            c2 = ci * 2 + h
            sq = etile.vr(etile.h[0:64, h * 1024:h * 1024 + 512],
                          0, 64, h * 1024, h * 1024 + 512)
            act(sq, kT2.v(0, 64, c2 * 512, (c2 + 1) * 512), AF.Square)
            mm(PSv(0, 1, 2560, 3072), ones64.v(0, 64, 0, 1), sq,
               start=True, stop=True)
            v_op("reduce_max", kmaxsq.v(0, 1, c2, c2 + 1), PSv(0, 1, 2560, 3072),
                 axis=AX.X)

    for ci in range(8):
        p1_chunk(x_full, ci * 1024, sp_xn, ci * 1024)
        p1_chunk(x_full, ci * 1024 + 512, sp_xn, ci * 1024 + 512)
        if ci >= 1:
            kv_chunk(ci - 1)

    if PHASES >= 2:
        for cj in range(4):
            p1_chunk(x_own, cj * 512, sp_xno, cj * 512)
    p1_flush()
    kv_chunk(7)

    em.cur_phase = 2
    for cj in range(2 if PHASES >= 3 else 0):
        xt = xnTb
        for ct in range(8):
            dma("scalar", xt.v(0, 128, ct * 1024, (ct + 1) * 1024),
                sp_xno.v(cj * 1024, (cj + 1) * 1024, ct * 128, (ct + 1) * 128),
                transpose=True)
        for h in range(2):
            for et in range(8):
                cc = (et % 2) * 512 + h * 1024
                for ct in range(8):
                    mm(PSv(0, 128, cc, cc + 512),
                       wqT.v(0, 128, ct * 1024 + et * 128, ct * 1024 + (et + 1) * 128),
                       xt.v(0, 128, ct * 1024 + h * 512, ct * 1024 + (h + 1) * 512),
                       start=(ct == 0), stop=(ct == 7))
                bq = (bias2.h[0:128, 16 + et:17 + et],
                      bias2.v(0, 128, 16 + et, 17 + et)[1])
                v_ts(qT.v(0, 128, et * OWN + cj * 1024 + h * 512,
                          et * OWN + cj * 1024 + (h + 1) * 512),
                     PSv(0, 128, cc, cc + 512), bq, None, AL.add)

    for sa in range(NSLOT if PHASES >= 3 else 0):
        qsl_ap = qT.h[0:128, :].rearrange("p (a b) -> p a b", a=NPAIR)[
            :, :, sa * CH:(sa + 1) * CH]
        qacc = ("SB", 0, 128, qT.off, qT.off + NPAIR * OWN * 2)
        sq_ap = PTb.h[0:128, 0:2048].rearrange("p (a b) -> p a b", a=NPAIR)
        sqv = PTb.vr(sq_ap, 0, 128, 0, 2048)
        act(sqv, (qsl_ap, qacc), AF.Square)
        for qc in range(4):
            cc = 2048 + (qc % 2) * 512
            mm(PSv(0, 1, cc, cc + 512), ones64.v(0, 64, 0, 1),
               PTb.vr(PTb.h[0:64, qc * 512:(qc + 1) * 512], 0, 64, qc * 512, (qc + 1) * 512),
               start=True, stop=False)
            mm(PSv(32, 33, cc, cc + 512), ones64.v(64, 128, 0, 1),
               PTb.vr(PTb.h[64:128, qc * 512:(qc + 1) * 512], 64, 128, qc * 512, (qc + 1) * 512),
               start=False, stop=True, tp=(64, 32))
            v_op("reduce_max", smallA.v(0, 33, qc, qc + 1), PSv(0, 33, cc, cc + 512),
                 axis=AX.X)
        v_op("reduce_max", smallA.v(0, 33, 4, 5), smallA.v(0, 33, 0, 4), axis=AX.X)
        dma("sync", sp_misc.v(0, 1, 0, 1), smallA.v(0, 1, 4, 5))
        dma("sync", sp_misc.v(0, 1, 1, 2), smallA.v(32, 33, 4, 5))
        dma("sync", smallA.v(0, 1, 8, 10), sp_misc.v(0, 1, 0, 2))
        v_op("reduce_max", smallA.v(0, 1, 10, 11), smallA.v(0, 1, 8, 10), axis=AX.X)
        v_op("reduce_max", smallA.v(0, 1, 11, 12), kmaxsq.v(0, 1, 0, 2 * sa + 2),
             axis=AX.X)
        v_tt(smallA.v(0, 1, 12, 13), smallA.v(0, 1, 10, 11),
             smallA.v(0, 1, 11, 12), AL.mult)
        act(smallA.v(0, 1, 13, 14), smallA.v(0, 1, 12, 13), AF.Ln,
            bias=(epsT.h[0:1, 0:1], epsT.v(0, 1, 0, 1)[1]))
        act(smallA.v(0, 1, 13, 14), smallA.v(0, 1, 13, 14), AF.Exp, scale=0.5)
        v_ts(smallA.v(0, 1, 14, 15), smallA.v(0, 1, 13, 14), -1.0, SHIFT_MARGIN,
             AL.mult, AL.add)
        v_op("tensor_copy", bcolB.v(0, 1, 0, 1), smallA.v(0, 1, 14, 15))
        mm(PSv(0, 128, 3072, 3073), onesrow.v(0, 1, 0, 128), bcolB.v(0, 1, 0, 1),
           start=True, stop=True)
        v_op("tensor_copy", biasT.v(0, 128, sa, sa + 1), PSv(0, 128, 3072, 3073))

    em.cur_phase = 3
    import collections
    BG = collections.deque()

    def bg_pump(n=1):
        for _ in range(n):
            if BG:
                BG.popleft()()

    def ffn_w_tile(src_in, sp_dst, ht, scale, seq):
        te = seq % 2
        wt_ap = xbuf.h.bitcast(I32)[0:128, te * 1024:(te + 1) * 1024]
        wt_i = xbuf.vr(wt_ap, 0, 128, te * 1024, (te + 1) * 1024)
        c1 = min(src_in.cols, 1024)
        dma("sync", wt_i, src_in.v(ht * 128, (ht + 1) * 128, 0, c1))
        stg = PTb.v(0, 128, te * 1024, (te + 1) * 1024)
        v_ts(stg, wt_i, scale, None, AL.mult)
        dma("sync", sp_dst.v(ht * 128, (ht + 1) * 128, 0, c1), stg)

    def p3_slot(sa):
        nJ = 8 * (sa + 1)
        for g in range(P3_GROUPS):
            p0 = 2 * g
            pvb = 2048
            rsb = 2560

            def sim(J):
                simc = (J % 2) * 1024
                for pi in range(2):
                    pair = p0 + pi
                    for hf in range(2):
                        oc = simc + hf * 512 + pi * 256
                        mm(PSv(0, 128, oc, oc + 256),
                           kT2.v(64 * hf, 64 * hf + 64, J * 128, (J + 1) * 128),
                           qT.v(64 * hf, 64 * hf + 64,
                                pair * OWN + sa * CH, pair * OWN + (sa + 1) * CH),
                           start=(pi == 0), stop=(pi == 1),
                           tp=(64 * hf, 0))

            def expmask(J):
                pp = J % 2
                simc = pp * 1024
                ptv = etile.v(0, 128, pp * 1024, (pp + 1) * 1024)
                act(ptv, PSv(0, 128, simc, simc + 1024), AF.Exp,
                    bias=(biasT.h[0:128, sa:sa + 1], biasT.v(0, 128, sa, sa + 1)[1]))
                r = J - 8 * sa
                if r >= 0:
                    m_ap = maskS.h[0:128, r * CH:(r + 1) * CH].unsqueeze(1) \
                        .broadcast_to((128, 4, CH))
                    macc = maskS.v(0, 128, r * CH, (r + 1) * CH)[1]
                    pt4 = etile.h[0:128, pp * 1024:(pp + 1) * 1024] \
                        .rearrange("p (a b) -> p a b", a=4)
                    em.add("vector", "c",
                           lambda e, o=pt4, m=m_ap: e.tensor_tensor(o, o, m, AL.mult),
                           ins=[ptv[1], macc], outs=[ptv[1]])

            def pvrs(J):
                pp = J % 2
                for pi in range(2):
                    for hf in range(2):
                        mm(PSv(64 * hf, 64 * hf + 64,
                               pvb + pi * 256, pvb + (pi + 1) * 256),
                           vaug.v(0, 128, J * 64, (J + 1) * 64),
                           etile.v(0, 128, pp * 1024 + hf * 512 + pi * 256,
                                   pp * 1024 + hf * 512 + (pi + 1) * 256),
                           start=(J == 0 and pi == 0), stop=(J == nJ - 1 and pi == 1),
                           tp=(0, 64 * hf))
                for pi in range(2):
                    for hf in range(2):
                        mm(PSv(64 * hf, 64 * hf + 64,
                               rsb + pi * 256, rsb + (pi + 1) * 256),
                           ones64.v(0, 128, 0, 64),
                           etile.v(0, 128, pp * 1024 + hf * 512 + pi * 256,
                                   pp * 1024 + hf * 512 + (pi + 1) * 256),
                           start=(J == 0 and pi == 0), stop=(J == nJ - 1 and pi == 1),
                           tp=(0, 64 * hf))

            sim(0)
            if nJ > 1:
                sim(1)
            for J in range(nJ):
                expmask(J)
                if J >= 1:
                    pvrs(J - 1)
                if J + 2 < nJ:
                    sim(J + 2)
                bg_pump(1)
            pvrs(nJ - 1)
            v_op("reciprocal", scr.v(0, 128, 3072, 3584),
                 PSv(0, 128, rsb, rsb + 512))
            for pi in range(2):
                pair = p0 + pi
                v_tt(oT.v(0, 128, pair * OWN + sa * CH, pair * OWN + (sa + 1) * CH),
                     PSv(0, 128, pvb + pi * 256, pvb + (pi + 1) * 256),
                     scr.v(0, 128, 3072 + pi * 256, 3072 + (pi + 1) * 256),
                     AL.mult)

    for et in range(8 if PHASES >= 5 else 0):
        dma("scalar", woutT.v(0, 128, et * 1024, (et + 1) * 1024),
            sp_wout.v(0, D, et * 128, (et + 1) * 128), transpose=True)

    def p4_tile_bg(cj, tt):
        te = (cj * 4 + tt) % 2
        tok0 = cj * 512 + tt * 128
        a0 = te * 1024
        w0 = 2048
        pieces = []
        pieces.append(lambda: dma(
            "gpsimd", xbuf.v(0, 128, te * 1024, (te + 1) * 1024),
            x_own.v(tok0, tok0 + 128, 0, D)))

        def mmp(cc):
            pc = 3072 + cc * 512
            for et in range(8):
                mm(PSv(0, 128, pc, pc + 512),
                   oT.v(0, 128, et * OWN + tok0, et * OWN + tok0 + 128),
                   woutT.v(0, 128, et * 1024 + cc * 512,
                           et * 1024 + (cc + 1) * 512),
                   start=(et == 0), stop=(et == 7))
        pieces.append(lambda: mmp(0))
        pieces.append(lambda: mmp(1))

        def evac():
            for cc in range(2):
                v_op("tensor_copy",
                     scr.v(0, 128, a0 + cc * 512, a0 + (cc + 1) * 512),
                     PSv(0, 128, 3072 + cc * 512, 3072 + (cc + 1) * 512))
            lnp_stats(scr, a0, te)
        pieces.append(evac)
        pieces.append(lambda: lnp_rstd(te))

        def apply1():
            lnp_apply(scr, a0, scr.v(0, 128, w0, w0 + 1024), te)
            v_tt(scr.v(0, 128, w0, w0 + 1024), scr.v(0, 128, w0, w0 + 1024),
                 lnoG.v(), AL.mult)
            v_tt(scr.v(0, 128, w0, w0 + 1024), scr.v(0, 128, w0, w0 + 1024),
                 lnoB.v(), AL.add)
            lnp_stats(scr, w0, te)
        pieces.append(apply1)
        pieces.append(lambda: lnp_rstd(te))

        def apply2():
            lnp_apply(scr, w0, scr.v(0, 128, a0, a0 + 1024), te)
            v_tt(scr.v(0, 128, a0, a0 + 1024), scr.v(0, 128, a0, a0 + 1024),
                 xbuf.v(0, 128, te * 1024, (te + 1) * 1024), AL.add)
            dma("sync", sp_x1.v(tok0, tok0 + 128, 0, D),
                scr.v(0, 128, a0, a0 + 1024))
            lnp_stats(scr, a0, te)
        pieces.append(apply2)
        pieces.append(lambda: lnp_rstd(te))

        def apply3():
            x1n = PTb.v(0, 128, te * 1024, (te + 1) * 1024)
            lnp_apply(scr, a0, x1n, te)
            dma("sync", sp_x1n.v(tok0, tok0 + 128, 0, D), x1n)
        pieces.append(apply3)
        return pieces

    if PHASES >= 4:
        for ht in range(NHT):
            BG.append(lambda ht=ht: ffn_w_tile(wg_in, sp_wg, ht, DQ, ht))
        for ct in range(8):
            BG.append(lambda ct=ct: dma(
                "scalar", wgT.v(0, 128, ct * HID, (ct + 1) * HID),
                sp_wg.v(0, HID, ct * 128, (ct + 1) * 128), transpose=True))
        def wd_tile(seq):
            ct, part = seq // 3, seq % 3
            te = seq % 2
            wt_ap = xbuf.h.bitcast(I32)[0:128, te * 1024:(te + 1) * 1024]
            wt_i = xbuf.vr(wt_ap, 0, 128, te * 1024, (te + 1) * 1024)
            dma("sync", wt_i,
                wd_in.v(ct * 128, (ct + 1) * 128, part * 1024, (part + 1) * 1024))
            stg = PTb.v(0, 128, te * 1024, (te + 1) * 1024)
            v_ts(stg, wt_i, DQ, None, AL.mult)
            dma("sync",
                sp_wd.v(ct * 128, (ct + 1) * 128, part * 1024, (part + 1) * 1024),
                stg)
        for seq in range(24):
            BG.append(lambda seq=seq: wd_tile(seq))

    for sa in range(min(NSLOT, P3_SLOTS) if PHASES >= 4 else 0):
        em.cur_phase = 3
        p3_slot(sa)
        em.cur_phase = 4
        if PHASES >= 5 and sa % 2 == 1:
            cj = sa // 2
            for tt in range(4):
                for p in p4_tile_bg(cj, tt):
                    BG.append(p)
            for k in range(6):
                ht = cj * 6 + k
                if ht < NHT:
                    BG.append(lambda ht=ht: ffn_w_tile(wu_in, sp_wu, ht, DQ, ht))
    while BG:
        bg_pump(1)

    for ct in range(8 if PHASES >= 6 else 0):
        dma("scalar", wuT.v(0, 128, ct * HID, (ct + 1) * HID),
            sp_wu.v(0, HID, ct * 128, (ct + 1) * 128), transpose=True)
    for ht in range(NHT if PHASES >= 6 else 0):
        dma("scalar", wdT.v(0, 128, ht * 1024, (ht + 1) * 1024),
            sp_wd.v(0, D, ht * 128, (ht + 1) * 128), transpose=True)

    em.cur_phase = 5
    for ck in range(8 if PHASES >= 6 else 0):
        tok0 = ck * CH
        for ct in range(8):
            dma(tq(), PTb.v(0, 128, ct * CH, (ct + 1) * CH),
                sp_x1n.v(tok0, tok0 + CH, ct * 128, (ct + 1) * 128),
                transpose=True)
        for ht in range(NHT):
            for ct in range(8):
                mm(PSv(0, 128, 0, 256),
                   wgT.v(0, 128, ct * HID + ht * 128, ct * HID + (ht + 1) * 128),
                   PTb.v(0, 128, ct * CH, (ct + 1) * CH),
                   start=(ct == 0), stop=(ct == 7))
            for ct in range(8):
                mm(PSv(0, 128, 512, 768),
                   wuT.v(0, 128, ct * HID + ht * 128, ct * HID + (ht + 1) * 128),
                   PTb.v(0, 128, ct * CH, (ct + 1) * CH),
                   start=(ct == 0), stop=(ct == 7))
            act(scr.v(0, 128, 3072, 3072 + CH), PSv(0, 128, 0, 256), AF.Silu)
            v_tt(hsbH.v(0, 128, ht * CH, (ht + 1) * CH),
                 scr.v(0, 128, 3072, 3072 + CH), PSv(0, 128, 512, 768), AL.mult)
        for tt in range(2):
            for cc in range(2):
                pcol = 1024 + tt * 1024 + cc * 512
                for ht in range(NHT):
                    mm(PSv(0, 128, pcol, pcol + 512),
                       hsbH.v(0, 128, ht * CH + tt * 128, ht * CH + tt * 128 + 128),
                       wdT.v(0, 128, ht * 1024 + cc * 512, ht * 1024 + (cc + 1) * 512),
                       start=(ht == 0), stop=(ht == NHT - 1))
        for tt in range(2):
            te = tt % 2
            pcol0 = 1024 + tt * 1024
            v_op("tensor_copy", scr.v(0, 128, te * 2048, te * 2048 + 512),
                 PSv(0, 128, pcol0, pcol0 + 512))
            v_op("tensor_copy", scr.v(0, 128, te * 2048 + 512, te * 2048 + 1024),
                 PSv(0, 128, pcol0 + 512, pcol0 + 1024))
            a0 = te * 2048
            o0 = te * 2048 + 1024
            ln_pass(scr, a0, scr.v(0, 128, o0, o0 + 1024), te)
            dma("gpsimd", xbuf.v(0, 128, te * 1024, (te + 1) * 1024),
                sp_x1.v(tok0 + tt * 128, tok0 + (tt + 1) * 128, 0, D))
            v_tt(scr.v(0, 128, o0, o0 + 1024), scr.v(0, 128, o0, o0 + 1024),
                 xbuf.v(0, 128, te * 1024, (te + 1) * 1024), AL.add)
            dma("sync", y_own.v(tok0 + tt * 128, tok0 + (tt + 1) * 128, 0, D),
                scr.v(0, 128, o0, o0 + 1024))

    if PHASES < 6:
        for t in range(OWN // 128):
            memset(scr.v(0, 128, 0, 1024), 0.0)
            dma("sync", y_own.v(t * 128, (t + 1) * 128, 0, D), scr.v(0, 128, 0, 1024))

    counters = em.resolve()
    sem_keys = sorted({(op.kind, op.eng) for op in em.ops if op.inc})
    by_eng = {}
    for op in em.ops:
        by_eng.setdefault(op.eng, []).append(op)

    sems = {}
    es = contextlib.ExitStack()
    with es:
        for key in sem_keys:
            sems[key] = es.enter_context(nc.semaphore(f"sem_{key[0]}_{key[1]}"))
        block = es.enter_context(nc.Block())

        def make_stream(ops, is_sync):
            def body(e):
                for op in ops:
                    for (key, val) in op.waits:
                        e.wait_ge(sems[key], val)
                    inst = op.emit(e)
                    if op.inc:
                        inst.then_inc(sems[(op.kind, op.eng)],
                                      16 if op.kind == "d" else 1)
                if is_sync:
                    for k in sorted(counters):
                        if k[0] == "d":
                            e.wait_ge(sems[k], counters[k])
            return body

        for eng_name in ("tensor", "vector", "scalar", "gpsimd", "sync"):
            ops = by_eng.get(eng_name, [])
            getattr(block, eng_name)(make_stream(ops, eng_name == "sync"))
    psum_ctx.close()
    return nc


_prog_cache = {}


def _get_program():
    if "nc" not in _prog_cache:
        _prog_cache["nc"] = build_program()
    return _prog_cache["nc"]


def _build_mask(cp):
    m = np.zeros((8, 128, CH), dtype=np.float32)
    j = np.arange(128)[:, None]
    q = np.arange(CH)[None, :]
    for r in range(8):
        if r < 2 * cp:
            m[r] = 1.0
        elif r < 2 * cp + 2:
            p = r - 2 * cp
            m[r] = (128 * p + j <= q).astype(np.float32)
    return np.ascontiguousarray(m.reshape(8 * 128, CH).astype(ml_dtypes.bfloat16))


def kernel(L0_attn_x, L0_attn_w_q, L0_attn_w_kv, L0_attn_w_out,
           L0_attn_ln_g, L0_attn_ln_b, L0_attn_lno_g, L0_attn_lno_b,
           L0_ffn_w_gate, L0_ffn_w_up, L0_ffn_w_down):
    x = np.ascontiguousarray(np.asarray(L0_attn_x, dtype=np.float32))
    wq = np.ascontiguousarray(np.asarray(L0_attn_w_q, dtype=np.float32))
    wkv = np.ascontiguousarray(np.asarray(L0_attn_w_kv, dtype=np.float32))
    wout = np.ascontiguousarray(np.asarray(L0_attn_w_out, dtype=np.float32))
    lngT = np.ascontiguousarray(
        np.asarray(L0_attn_ln_g, dtype=np.float32).reshape(8, 128).T)
    lnbT = np.ascontiguousarray(
        np.asarray(L0_attn_ln_b, dtype=np.float32).reshape(8, 128).T)
    lnog = np.ascontiguousarray(np.broadcast_to(
        np.asarray(L0_attn_lno_g, dtype=np.float32)[None, :], (128, D)))
    lnob = np.ascontiguousarray(np.broadcast_to(
        np.asarray(L0_attn_lno_b, dtype=np.float32)[None, :], (128, D)))
    wg = np.ascontiguousarray(np.asarray(L0_ffn_w_gate, dtype=np.int32))
    wu = np.ascontiguousarray(np.asarray(L0_ffn_w_up, dtype=np.int32))
    wd = np.ascontiguousarray(np.asarray(L0_ffn_w_down, dtype=np.int32))

    nc = _get_program()
    in_maps = []
    for core in range(8):
        b = core // 4
        cp = core % 4
        ids = [4 * a + cp for a in range(NSLOT)]
        xb = np.ascontiguousarray(x[b])
        x_own_np = np.ascontiguousarray(
            np.concatenate([xb[k * CH:(k + 1) * CH] for k in ids], axis=0))
        in_maps.append({
            "x_full": xb,
            "x_own": x_own_np,
            "wq": wq, "wkv": wkv, "wout": wout,
            "lngT": lngT, "lnbT": lnbT, "lnog": lnog, "lnob": lnob,
            "wg": wg, "wu": wu, "wd": wd,
            "maskt": _build_mask(cp),
        })
    res = run_bass_kernel_spmd(nc, in_maps, core_ids=list(range(8)))
    global _last_res
    _last_res = res
    out = np.empty((B, N, D), dtype=np.float32)
    for core in range(8):
        b = core // 4
        cp = core % 4
        yo = np.asarray(res.results[core]["y_own"])
        for a in range(NSLOT):
            k = 4 * a + cp
            out[b, k * CH:(k + 1) * CH] = yo[a * CH:(a + 1) * CH]
    return out


# version 12 (final)
# speedup vs baseline: 1.2247x; 1.0008x over previous
import contextlib
import numpy as np
import ml_dtypes
import concourse.bass as bass
import concourse.mybir as mybir
from concourse.bass_utils import run_bass_kernel_spmd

F32 = mybir.dt.float32
BF16 = mybir.dt.bfloat16
I32 = mybir.dt.int32
AF = mybir.ActivationFunctionType
AL = mybir.AluOpType
AX = mybir.AxisListType

B = 2
N = 8192
D = 1024
H = 16
DH = 64
HID = 3072
NPAIR = H // 2
CH = 256
NSLOT = 8
OWN = NSLOT * CH
NCT = D // 128
NHT = HID // 128
EPS = 1e-5
QSCALE = 1.0 / 64.0
DQ = 1.0 / 127.0
SHIFT_MARGIN = 30.0

ESZ = {F32: 4, BF16: 2, I32: 4}

PARANOID = False
PARANOID_PHASES = None
PHASES = 99
DBG = set()
P3_SLOTS = 99
P3_GROUPS = 4
P3_PARTS = {"sim", "exp", "mask", "pv", "rs", "norm"}


class _Op:
    __slots__ = ("idx", "eng", "kind", "emit", "deps", "inc", "count", "waits",
                 "phase")

    def __init__(self, idx, eng, kind, emit, deps):
        self.phase = 0
        self.idx = idx
        self.eng = eng
        self.kind = kind
        self.emit = emit
        self.deps = deps
        self.inc = kind == "d"
        self.count = None
        self.waits = []


class Em:
    def __init__(self):
        self.ops = []
        self.wstate = {}
        self.rstate = {}
        self.cur_phase = 0

    @staticmethod
    def _overlap(a, b):
        return a[1] > b[0] and b[1] > a[0] and a[3] > b[2] and b[3] > a[2]

    def _scan(self, state, acc):
        lst = state.get(acc[0])
        if not lst:
            return ()
        r = acc[1:]
        return tuple(e[4] for e in lst if self._overlap(r, e))

    def add(self, eng, kind, emit, ins=(), outs=()):
        idx = len(self.ops)
        deps = set()
        for acc in ins:
            deps.update(self._scan(self.wstate, acc))
        for acc in outs:
            deps.update(self._scan(self.wstate, acc))
            deps.update(self._scan(self.rstate, acc))
        deps.discard(idx)
        for acc in ins:
            self.rstate.setdefault(acc[0], []).append(
                (acc[1], acc[2], acc[3], acc[4], idx))
        for acc in outs:
            key, p0, p1, b0, b1 = acc
            for state in (self.wstate, self.rstate):
                lst = state.get(key)
                if lst:
                    lst[:] = [e for e in lst
                              if not (p0 <= e[0] and e[1] <= p1
                                      and b0 <= e[2] and e[3] <= b1)]
            self.wstate.setdefault(key, []).append((p0, p1, b0, b1, idx))
        op = _Op(idx, eng, kind, emit, deps)
        op.phase = self.cur_phase
        self.ops.append(op)
        return op

    def resolve(self):
        ops = self.ops
        for op in ops:
            for d in op.deps:
                dop = ops[d]
                if dop.eng == "tensor" and op.eng == "tensor" \
                        and dop.kind == "c" and op.kind == "c":
                    continue
                dop.inc = True
        counters = {}
        for op in ops:
            if op.inc:
                key = (op.kind, op.eng)
                counters[key] = counters.get(key, 0) + (16 if op.kind == "d" else 1)
                op.count = counters[key]
        stream_guar = {}
        last_count = {}
        for op in ops:
            base = stream_guar.setdefault(op.eng, {})
            needed = {}
            if PARANOID and (PARANOID_PHASES is None
                             or op.phase in PARANOID_PHASES):
                for key, val in last_count.items():
                    if key[1] != op.eng or key[0] != op.kind:
                        needed[key] = val
            for d in op.deps:
                dop = ops[d]
                if dop.eng == "tensor" and op.eng == "tensor" \
                        and dop.kind == "c" and op.kind == "c":
                    continue
                key = (dop.kind, dop.eng)
                if needed.get(key, -1) < dop.count:
                    needed[key] = dop.count
            if op.inc:
                last_count[(op.kind, op.eng)] = op.count
            waits = []
            for key, val in sorted(needed.items()):
                if base.get(key, -1) < val:
                    waits.append((key, val))
                    base[key] = val
            op.waits = waits
        return counters


SB_BASE = 24576


class SBuf:
    def __init__(self, nc, name, cols, dtype, off):
        off = off + SB_BASE
        self.h = nc.alloc_sbuf_tensor_at(name, [128, cols], dtype, offset=off)
        self.esz = ESZ[dtype]
        self.off = off
        self.cols = cols

    def v(self, p0=0, p1=128, c0=0, c1=None):
        if c1 is None:
            c1 = self.cols
        ap = self.h[p0:p1, c0:c1]
        acc = ("SB", p0, p1, self.off + c0 * self.esz, self.off + c1 * self.esz)
        return ap, acc

    def vr(self, ap, p0, p1, c0, c1):
        return ap, ("SB", p0, p1, self.off + c0 * self.esz, self.off + c1 * self.esz)


class DBuf:
    def __init__(self, nc, name, rows, cols, dtype, kind="Internal"):
        if kind == "Internal" and name in DBG:
            kind = "ExternalOutput"
        self.h = nc.dram_tensor(name, [rows, cols], dtype, kind=kind)
        self.esz = ESZ[dtype]
        self.name = name
        self.rows = rows
        self.cols = cols

    def v(self, r0=0, r1=None, c0=0, c1=None):
        if r1 is None:
            r1 = self.rows
        if c1 is None:
            c1 = self.cols
        ap = self.h.ap()[r0:r1, c0:c1]
        acc = ("DR:" + self.name, r0, r1, c0 * self.esz, c1 * self.esz)
        return ap, acc


def build_program():
    nc = bass.Bass("TRN2", target_bir_lowering=False)
    em = Em()

    x_full = DBuf(nc, "x_full", N, D, F32, kind="ExternalInput")
    x_own = DBuf(nc, "x_own", OWN, D, F32, kind="ExternalInput")
    wq_in = DBuf(nc, "wq", D, D, F32, kind="ExternalInput")
    wkv_in = DBuf(nc, "wkv", 2 * DH, D, F32, kind="ExternalInput")
    lngT_in = DBuf(nc, "lngT", 128, 8, F32, kind="ExternalInput")
    lnbT_in = DBuf(nc, "lnbT", 128, 8, F32, kind="ExternalInput")
    wout_in = DBuf(nc, "wout", D, D, F32, kind="ExternalInput")
    lnog_in = DBuf(nc, "lnog", 128, D, F32, kind="ExternalInput")
    lnob_in = DBuf(nc, "lnob", 128, D, F32, kind="ExternalInput")
    wg_in = DBuf(nc, "wg", HID, D, I32, kind="ExternalInput")
    wu_in = DBuf(nc, "wu", HID, D, I32, kind="ExternalInput")
    wd_in = DBuf(nc, "wd", D, HID, I32, kind="ExternalInput")
    mask_in = DBuf(nc, "maskt", 8 * 128, CH, BF16, kind="ExternalInput")
    y_own = DBuf(nc, "y_own", OWN, D, F32, kind="ExternalOutput")

    sp_xn = DBuf(nc, "sp_xn", N, D, BF16)
    sp_xno = DBuf(nc, "sp_xno", OWN, D, BF16)
    sp_wq = DBuf(nc, "sp_wq", D, D, BF16)
    sp_wkv = DBuf(nc, "sp_wkv", 2 * DH, D, BF16)
    sp_wout = DBuf(nc, "sp_wout", D, D, BF16)
    sp_wg = DBuf(nc, "sp_wg", HID, D, BF16)
    sp_wu = DBuf(nc, "sp_wu", HID, D, BF16)
    sp_wd = DBuf(nc, "sp_wd", D, HID, BF16)
    sp_x1 = DBuf(nc, "sp_x1", OWN, D, F32)
    sp_x1n = DBuf(nc, "sp_x1n", OWN, D, BF16)
    sp_misc = DBuf(nc, "sp_misc", 8, 8, F32)

    maskS = SBuf(nc, "maskS", 8 * CH, BF16, 0)
    biasT = SBuf(nc, "biasT", 16, F32, 4096)
    ones64 = SBuf(nc, "ones64", 128, BF16, 4160)
    onesrow = SBuf(nc, "onesrow", 128, BF16, 4416)
    epsT = SBuf(nc, "epsT", 8, F32, 4672)
    kmaxsq = SBuf(nc, "kmaxsq", 16, F32, 4704)
    smallA = SBuf(nc, "smallA", 32, F32, 4768)
    bcolB = SBuf(nc, "bcolB", 8, BF16, 4896)
    stats = SBuf(nc, "stats", 40, F32, 4928)
    bias2 = SBuf(nc, "bias2", 96, F32, 5088)
    lnbTb = SBuf(nc, "lnbTb", 8, BF16, 5472)
    PTb = SBuf(nc, "PTb", 2048, BF16, 5536)
    xbuf = SBuf(nc, "xbuf", 2048, F32, 9632)
    scr = SBuf(nc, "scr", 4096, F32, 17824)
    etile = SBuf(nc, "etile", 2048, BF16, 34208)
    lnoG = SBuf(nc, "lnoG", D, F32, 39328)
    lnoB = SBuf(nc, "lnoB", D, F32, 43424)
    vaug = SBuf(nc, "vaug", 4096, BF16, 47520)
    wqT = SBuf(nc, "wqT", 8 * D, BF16, 55712)
    wkvT = SBuf(nc, "wkvT", 8 * 128, BF16, 72096)
    xnTb = SBuf(nc, "xnTb", 8 * 1024, BF16, 74144)
    wgT = SBuf(nc, "wgT", 8 * HID, BF16, 55712)
    qT = SBuf(nc, "qT", NPAIR * OWN, BF16, 104864)
    kT2 = SBuf(nc, "kT2", N, BF16, 137632)
    wuT = SBuf(nc, "wuT", 8 * HID, BF16, 104864)
    oT = SBuf(nc, "oT", NPAIR * OWN, BF16, 154016)
    woutT = SBuf(nc, "woutT", 8 * D, BF16, 186784)
    wdT = SBuf(nc, "wdT", NHT * D, BF16, 154016)
    hsbH = SBuf(nc, "hsbH", NHT * CH, BF16, 34208)

    psum_ctx = contextlib.ExitStack()
    ps_h = psum_ctx.enter_context(nc.psum_tensor("ps", [128, 4096], F32))

    def PSv(p0, p1, c0, c1):
        ap = ps_h[p0:p1, c0:c1]
        b0 = (c0 // 512) * 2048
        b1 = ((c1 + 511) // 512) * 2048
        return ap, ("PS", 0, 128, b0, b1)

    def mm(out_v, lhs_v, rhs_v, start, stop, tp=None):
        (oap, oacc) = out_v
        (lap, lacc) = lhs_v
        (rap, racc) = rhs_v
        kw = {}
        if tp is not None:
            kw["tile_position"] = tp
        em.add("tensor", "c",
               lambda e, oap=oap, lap=lap, rap=rap, kw=kw, start=start, stop=stop:
               e.matmul(oap, lap, rap, start=start, stop=stop, **kw),
               ins=[lacc, racc], outs=[oacc])

    def act(out_v, in_v, func, bias=0.0, scale=1.0):
        (oap, oacc) = out_v
        (iap, iacc) = in_v
        ins = [iacc]
        bap = bias
        if isinstance(bias, tuple):
            bap = bias[0]
            ins.append(bias[1])
        em.add("scalar", "c",
               lambda e, oap=oap, iap=iap, func=func, bap=bap, scale=scale:
               e.activation(oap, iap, func, bias=bap, scale=scale),
               ins=ins, outs=[oacc])

    def v_op(fn_name, out_v, *in_vs, **kw):
        (oap, oacc) = out_v
        iaps = [iv[0] for iv in in_vs]
        iaccs = [iv[1] for iv in in_vs]

        def emit(e, fn_name=fn_name, oap=oap, iaps=iaps, kw=kw):
            return getattr(e, fn_name)(oap, *iaps, **kw)
        em.add("vector", "c", emit, ins=iaccs, outs=[oacc])

    def v_ts(out_v, in_v, s1, s2, op0, op1=None):
        (oap, oacc) = out_v
        (iap, iacc) = in_v
        ins = [iacc]
        s1ap, s2ap = s1, s2
        if isinstance(s1, tuple):
            s1ap = s1[0]
            ins.append(s1[1])
        if isinstance(s2, tuple):
            s2ap = s2[0]
            ins.append(s2[1])
        kw = {}
        if op1 is not None:
            kw["op1"] = op1
        em.add("vector", "c",
               lambda e, oap=oap, iap=iap, s1ap=s1ap, s2ap=s2ap, op0=op0, kw=kw:
               e.tensor_scalar(oap, iap, s1ap, s2ap, op0, **kw),
               ins=ins, outs=[oacc])

    def v_tt(out_v, a_v, b_v, op):
        v_op("tensor_tensor", out_v, a_v, b_v, op=op)

    def dma(queue, out_v, in_v, transpose=False):
        (oap, oacc) = out_v
        (iap, iacc) = in_v
        if transpose:
            em.add(queue, "d",
                   lambda e, oap=oap, iap=iap: e.dma_start_transpose(oap, iap),
                   ins=[iacc], outs=[oacc])
        else:
            em.add(queue, "d",
                   lambda e, oap=oap, iap=iap: e.dma_start(out=oap, in_=iap),
                   ins=[iacc], outs=[oacc])

    _tq = [0]

    def tq():
        _tq[0] ^= 1
        return "scalar"

    def memset(out_v, val):
        (oap, oacc) = out_v
        em.add("vector", "c", lambda e, oap=oap, val=val: e.memset(oap, val),
               ins=[], outs=[oacc])

    memset(epsT.v(0, 128, 0, 1), EPS)
    memset(epsT.v(0, 128, 1, 2), EPS * EPS)
    memset(ones64.v(0, 128, 0, 64), 1.0)
    memset(ones64.v(0, 128, 64, 128), 1.0)
    memset(onesrow.v(0, 1, 0, 128), 1.0)

    for r in range(8):
        dma("gpsimd", maskS.v(0, 128, r * CH, (r + 1) * CH),
            mask_in.v(r * 128, (r + 1) * 128, 0, CH))
    dma("gpsimd", bias2.v(0, 128, 0, 8), lngT_in.v())
    dma("gpsimd", bias2.v(0, 128, 8, 16), lnbT_in.v())
    dma("gpsimd", lnoG.v(), lnog_in.v())
    dma("gpsimd", lnoB.v(), lnob_in.v())

    for et in range(8):
        te = et % 2
        wt_f = scr.v(0, 128, te * 1024, (te + 1) * 1024)
        dma("sync", wt_f, wq_in.v(et * 128, (et + 1) * 128, 0, D))
        stg = PTb.v(0, 128, te * 1024, (te + 1) * 1024)
        v_ts(stg, wt_f, QSCALE, None, AL.mult)
        dma("sync", sp_wq.v(et * 128, (et + 1) * 128, 0, D), stg)
    for ct in range(8):
        dma("scalar", wqT.v(0, 128, ct * 1024, (ct + 1) * 1024),
            sp_wq.v(0, D, ct * 128, (ct + 1) * 128), transpose=True)

    dma("gpsimd", sp_wkv.v(), wkv_in.v())
    for ct in range(8):
        dma("scalar", wkvT.v(0, 128, ct * 128, (ct + 1) * 128),
            sp_wkv.v(0, 2 * DH, ct * 128, (ct + 1) * 128), transpose=True)

    dma("gpsimd", sp_wout.v(), wout_in.v())

    v_op("tensor_copy", lnbTb.v(0, 128, 0, 8), bias2.v(0, 128, 8, 16))
    for et in range(8):
        for ct in range(8):
            mm(PSv(0, 128, 3072 + et, 3073 + et),
               wqT.v(0, 128, ct * 1024 + et * 128, ct * 1024 + (et + 1) * 128),
               lnbTb.vr(lnbTb.h[0:128, ct:ct + 1], 0, 128, ct, ct + 1),
               start=(ct == 0), stop=(ct == 7))
    for ct in range(8):
        mm(PSv(0, 128, 3080, 3081),
           wkvT.v(0, 128, ct * 128, (ct + 1) * 128),
           lnbTb.vr(lnbTb.h[0:128, ct:ct + 1], 0, 128, ct, ct + 1),
           start=(ct == 0), stop=(ct == 7))
        mm(PSv(64, 128, 3081, 3082),
           wkvT.v(0, 128, ct * 128, ct * 128 + 64),
           lnbTb.vr(lnbTb.h[0:128, ct:ct + 1], 0, 128, ct, ct + 1),
           start=(ct == 0), stop=(ct == 7), tp=(0, 64))
    v_op("tensor_copy", bias2.v(0, 128, 16, 26), PSv(0, 128, 3072, 3082))
    for ct in range(8):
        mm(PSv(0, 1, 3584, 3712),
           lnbTb.vr(lnbTb.h[0:128, ct:ct + 1], 0, 128, ct, ct + 1),
           wkvT.v(0, 128, ct * 128, (ct + 1) * 128),
           start=(ct == 0), stop=(ct == 7))
    v_op("tensor_copy", etile.v(0, 1, 0, 128), PSv(0, 1, 3584, 3712))
    mm(PSv(0, 128, 3712, 3776), onesrow.v(0, 1, 0, 128), etile.v(0, 1, 64, 128),
       start=True, stop=True)
    v_op("tensor_copy", bias2.v(0, 128, 32, 96), PSv(0, 128, 3712, 3776))
    for ct in range(8):
        gcol = (bias2.h[0:128, ct:ct + 1], bias2.v(0, 128, ct, ct + 1)[1])
        v_ts(wqT.v(0, 128, ct * 1024, (ct + 1) * 1024),
             wqT.v(0, 128, ct * 1024, (ct + 1) * 1024), gcol, None, AL.mult)
        v_ts(wkvT.v(0, 128, ct * 128, (ct + 1) * 128),
             wkvT.v(0, 128, ct * 128, (ct + 1) * 128), gcol, None, AL.mult)

    em.cur_phase = 1
    def lnp_stats(src_buf, sc0, sset):
        base = sset * 20
        s6 = stats.v(0, 128, base, base + 12)
        s6a = stats.v(0, 128, base, base + 6)
        s6b = stats.v(0, 128, base + 6, base + 12)
        s2 = stats.v(0, 128, base + 12, base + 14)
        v_op("bn_stats", s6a, src_buf.v(0, 128, sc0, sc0 + 512))
        v_op("bn_stats", s6b, src_buf.v(0, 128, sc0 + 512, sc0 + 1024))
        v_op("bn_aggr", s2, s6)

    def lnp_rstd(sset):
        base = sset * 20
        var = stats.v(0, 128, base + 13, base + 14)
        rs = stats.v(0, 128, base + 14, base + 15)
        act(rs, var, AF.Ln, bias=(epsT.h[0:128, 0:1], epsT.v(0, 128, 0, 1)[1]))
        act(rs, rs, AF.Exp, scale=-0.5)

    def lnp_apply(src_buf, sc0, dst_v, sset):
        base = sset * 20
        mu = stats.v(0, 128, base + 12, base + 13)
        v_ts(dst_v, src_buf.v(0, 128, sc0, sc0 + 1024),
             (stats.h[0:128, base + 12:base + 13], mu[1]),
             (stats.h[0:128, base + 14:base + 15],
              stats.v(0, 128, base + 14, base + 15)[1]),
             AL.subtract, AL.mult)

    def ln_pass(src_buf, sc0, dst_v, sset, affine=None):
        lnp_stats(src_buf, sc0, sset)
        lnp_rstd(sset)
        lnp_apply(src_buf, sc0, dst_v, sset)

    def ln2_to_bf16(x_buf, xc0, out_v, sset):
        base = sset * 20
        s6 = stats.v(0, 128, base, base + 12)
        s6a = stats.v(0, 128, base, base + 6)
        s6b = stats.v(0, 128, base + 6, base + 12)
        s2 = stats.v(0, 128, base + 12, base + 14)
        var = stats.v(0, 128, base + 13, base + 14)
        mu = stats.v(0, 128, base + 12, base + 13)
        rs = stats.v(0, 128, base + 14, base + 15)
        v_op("bn_stats", s6a, x_buf.v(0, 128, xc0, xc0 + 512))
        v_op("bn_stats", s6b, x_buf.v(0, 128, xc0 + 512, xc0 + 1024))
        v_op("bn_aggr", s2, s6)
        act(rs, var, AF.Ln, scale=1.0 + EPS,
            bias=(epsT.h[0:128, 1:2], epsT.v(0, 128, 1, 2)[1]))
        act(rs, rs, AF.Exp, scale=-0.5)
        v_ts(out_v, x_buf.v(0, 128, xc0, xc0 + 1024),
             (stats.h[0:128, base + 12:base + 13], mu[1]),
             (stats.h[0:128, base + 14:base + 15], rs[1]),
             AL.subtract, AL.mult)

    _tile = [0]
    _pend = [None]

    def p1_tile(src_dbuf, srow, dst_dbuf, drow):
        ti = _tile[0]
        _tile[0] += 1
        t4 = ti % 4
        te = ti % 2
        dma("sync", scr.v(0, 128, t4 * 1024, (t4 + 1) * 1024),
            src_dbuf.v(srow, srow + 128, 0, D))
        base = te * 20
        s6a = stats.v(0, 128, base, base + 6)
        s6b = stats.v(0, 128, base + 6, base + 12)
        s2 = stats.v(0, 128, base + 12, base + 14)
        var = stats.v(0, 128, base + 13, base + 14)
        rs = stats.v(0, 128, base + 14, base + 15)
        v_op("bn_stats", s6a, scr.v(0, 128, t4 * 1024, t4 * 1024 + 512))
        v_op("bn_stats", s6b, scr.v(0, 128, t4 * 1024 + 512, t4 * 1024 + 1024))
        v_op("bn_aggr", s2, stats.v(0, 128, base, base + 12))
        act(rs, var, AF.Ln, scale=1.0 + EPS,
            bias=(epsT.h[0:128, 1:2], epsT.v(0, 128, 1, 2)[1]))
        act(rs, rs, AF.Exp, scale=-0.5)
        prev = _pend[0]

        def apply():
            out = PTb.v(0, 128, te * 1024, (te + 1) * 1024)
            v_ts(out, scr.v(0, 128, t4 * 1024, (t4 + 1) * 1024),
                 (stats.h[0:128, base + 12:base + 13],
                  stats.v(0, 128, base + 12, base + 13)[1]),
                 (stats.h[0:128, base + 14:base + 15],
                  stats.v(0, 128, base + 14, base + 15)[1]),
                 AL.subtract, AL.mult)
            dma("gpsimd", dst_dbuf.v(drow, drow + 128, 0, D), out)
        _pend[0] = apply
        if prev:
            prev()

    def p1_flush():
        if _pend[0]:
            _pend[0]()
            _pend[0] = None

    def p1_chunk(src_dbuf, src_row0, dst_dbuf, dst_row0):
        for t in range(4):
            p1_tile(src_dbuf, src_row0 + t * 128, dst_dbuf, dst_row0 + t * 128)

    def kv_chunk(ci):
        xt = xnTb
        for ct in range(8):
            dma("scalar", xt.v(0, 128, ct * 1024, (ct + 1) * 1024),
                sp_xn.v(ci * 1024, (ci + 1) * 1024, ct * 128, (ct + 1) * 128),
                transpose=True)
        for h in range(2):
            cA = h * 512
            cB = 1024 + h * 512
            for ct in range(8):
                mm(PSv(0, 128, cA, cA + 512),
                   wkvT.v(0, 128, ct * 128, (ct + 1) * 128),
                   xt.v(0, 128, ct * 1024 + h * 512, ct * 1024 + (h + 1) * 512),
                   start=(ct == 0), stop=(ct == 7))
            for ct in range(8):
                mm(PSv(64, 128, cB, cB + 512),
                   wkvT.v(0, 128, ct * 128, ct * 128 + 64),
                   xt.v(0, 128, ct * 1024 + h * 512, ct * 1024 + (h + 1) * 512),
                   start=(ct == 0), stop=(ct == 7), tp=(0, 64))
        bk0 = (bias2.h[0:64, 24:25], bias2.v(0, 64, 24, 25)[1])
        bk1 = (bias2.h[64:128, 25:26], bias2.v(64, 128, 25, 26)[1])
        kv0 = (ps_h[0:64, 0:1024], PSv(0, 64, 0, 1024)[1])
        kv1 = (ps_h[64:128, 1024:2048], PSv(64, 128, 1024, 2048)[1])
        em.add("vector", "c",
               lambda e, o=kT2.h[0:64, ci * 1024:(ci + 1) * 1024], i=kv0[0], s=bk0[0]:
               e.tensor_scalar(o, i, s, None, AL.add),
               ins=[kv0[1], bk0[1]],
               outs=[kT2.v(0, 64, ci * 1024, (ci + 1) * 1024)[1]])
        em.add("vector", "c",
               lambda e, o=kT2.h[64:128, ci * 1024:(ci + 1) * 1024], i=kv1[0], s=bk1[0]:
               e.tensor_scalar(o, i, s, None, AL.add),
               ins=[kv1[1], bk1[1]],
               outs=[kT2.v(64, 128, ci * 1024, (ci + 1) * 1024)[1]])
        for tt in range(8):
            for ct in range(8):
                mm(PSv(0, 128, 2048 + tt * 64, 2048 + (tt + 1) * 64),
                   xt.vr(xt.h[0:128, ct * 1024 + tt * 128: ct * 1024 + (tt + 1) * 128],
                         0, 128, ct * 1024 + tt * 128, ct * 1024 + (tt + 1) * 128),
                   wkvT.v(0, 128, ct * 128 + 64, (ct + 1) * 128),
                   start=(ct == 0), stop=(ct == 7))
        bv3 = bias2.h[0:128, 32:96].unsqueeze(1).broadcast_to((128, 8, 64))
        va3 = vaug.h[0:128, ci * 512:(ci + 1) * 512].rearrange(
            "p (a b) -> p a b", a=8)
        pv3 = ps_h[0:128, 2048:2560].rearrange("p (a b) -> p a b", a=8)
        em.add("vector", "c",
               lambda e, o=va3, i=pv3, s=bv3: e.tensor_tensor(o, i, s, AL.add),
               ins=[PSv(0, 128, 2048, 2560)[1], bias2.v(0, 128, 32, 96)[1]],
               outs=[vaug.v(0, 128, ci * 512, (ci + 1) * 512)[1]])
        for h in range(2):
            c2 = ci * 2 + h
            sq = etile.vr(etile.h[0:64, h * 1024:h * 1024 + 512],
                          0, 64, h * 1024, h * 1024 + 512)
            act(sq, kT2.v(0, 64, c2 * 512, (c2 + 1) * 512), AF.Square)
            mm(PSv(0, 1, 2560, 3072), ones64.v(0, 64, 0, 1), sq,
               start=True, stop=True)
            v_op("reduce_max", kmaxsq.v(0, 1, c2, c2 + 1), PSv(0, 1, 2560, 3072),
                 axis=AX.X)

    for ci in range(8):
        p1_chunk(x_full, ci * 1024, sp_xn, ci * 1024)
        p1_chunk(x_full, ci * 1024 + 512, sp_xn, ci * 1024 + 512)
        if ci >= 1:
            kv_chunk(ci - 1)

    if PHASES >= 2:
        for cj in range(4):
            p1_chunk(x_own, cj * 512, sp_xno, cj * 512)
    p1_flush()
    kv_chunk(7)

    em.cur_phase = 2
    for cj in range(2 if PHASES >= 3 else 0):
        xt = xnTb
        for ct in range(8):
            dma("scalar", xt.v(0, 128, ct * 1024, (ct + 1) * 1024),
                sp_xno.v(cj * 1024, (cj + 1) * 1024, ct * 128, (ct + 1) * 128),
                transpose=True)
        for h in range(2):
            for et in range(8):
                cc = (et % 2) * 512 + h * 1024
                for ct in range(8):
                    mm(PSv(0, 128, cc, cc + 512),
                       wqT.v(0, 128, ct * 1024 + et * 128, ct * 1024 + (et + 1) * 128),
                       xt.v(0, 128, ct * 1024 + h * 512, ct * 1024 + (h + 1) * 512),
                       start=(ct == 0), stop=(ct == 7))
                bq = (bias2.h[0:128, 16 + et:17 + et],
                      bias2.v(0, 128, 16 + et, 17 + et)[1])
                v_ts(qT.v(0, 128, et * OWN + cj * 1024 + h * 512,
                          et * OWN + cj * 1024 + (h + 1) * 512),
                     PSv(0, 128, cc, cc + 512), bq, None, AL.add)

    for sa in range(NSLOT if PHASES >= 3 else 0):
        qsl_ap = qT.h[0:128, :].rearrange("p (a b) -> p a b", a=NPAIR)[
            :, :, sa * CH:(sa + 1) * CH]
        qacc = ("SB", 0, 128, qT.off, qT.off + NPAIR * OWN * 2)
        sq_ap = PTb.h[0:128, 0:2048].rearrange("p (a b) -> p a b", a=NPAIR)
        sqv = PTb.vr(sq_ap, 0, 128, 0, 2048)
        act(sqv, (qsl_ap, qacc), AF.Square)
        for qc in range(4):
            cc = 2048 + (qc % 2) * 512
            mm(PSv(0, 1, cc, cc + 512), ones64.v(0, 64, 0, 1),
               PTb.vr(PTb.h[0:64, qc * 512:(qc + 1) * 512], 0, 64, qc * 512, (qc + 1) * 512),
               start=True, stop=False)
            mm(PSv(32, 33, cc, cc + 512), ones64.v(64, 128, 0, 1),
               PTb.vr(PTb.h[64:128, qc * 512:(qc + 1) * 512], 64, 128, qc * 512, (qc + 1) * 512),
               start=False, stop=True, tp=(64, 32))
            v_op("reduce_max", smallA.v(0, 33, qc, qc + 1), PSv(0, 33, cc, cc + 512),
                 axis=AX.X)
        v_op("reduce_max", smallA.v(0, 33, 4, 5), smallA.v(0, 33, 0, 4), axis=AX.X)
        dma("sync", sp_misc.v(0, 1, 0, 1), smallA.v(0, 1, 4, 5))
        dma("sync", sp_misc.v(0, 1, 1, 2), smallA.v(32, 33, 4, 5))
        dma("sync", smallA.v(0, 1, 8, 10), sp_misc.v(0, 1, 0, 2))
        v_op("reduce_max", smallA.v(0, 1, 10, 11), smallA.v(0, 1, 8, 10), axis=AX.X)
        v_op("reduce_max", smallA.v(0, 1, 11, 12), kmaxsq.v(0, 1, 0, 2 * sa + 2),
             axis=AX.X)
        v_tt(smallA.v(0, 1, 12, 13), smallA.v(0, 1, 10, 11),
             smallA.v(0, 1, 11, 12), AL.mult)
        act(smallA.v(0, 1, 13, 14), smallA.v(0, 1, 12, 13), AF.Ln,
            bias=(epsT.h[0:1, 0:1], epsT.v(0, 1, 0, 1)[1]))
        act(smallA.v(0, 1, 13, 14), smallA.v(0, 1, 13, 14), AF.Exp, scale=0.5)
        v_ts(smallA.v(0, 1, 14, 15), smallA.v(0, 1, 13, 14), -1.0, SHIFT_MARGIN,
             AL.mult, AL.add)
        v_op("tensor_copy", bcolB.v(0, 1, 0, 1), smallA.v(0, 1, 14, 15))
        mm(PSv(0, 128, 3072, 3073), onesrow.v(0, 1, 0, 128), bcolB.v(0, 1, 0, 1),
           start=True, stop=True)
        v_op("tensor_copy", biasT.v(0, 128, sa, sa + 1), PSv(0, 128, 3072, 3073))

    em.cur_phase = 3
    import collections
    BG = collections.deque()

    def bg_pump(n=1):
        for _ in range(n):
            if BG:
                BG.popleft()()

    def ffn_w_tile(src_in, sp_dst, ht, scale, seq):
        te = seq % 2
        wt_ap = xbuf.h.bitcast(I32)[0:128, te * 1024:(te + 1) * 1024]
        wt_i = xbuf.vr(wt_ap, 0, 128, te * 1024, (te + 1) * 1024)
        c1 = min(src_in.cols, 1024)
        dma("sync", wt_i, src_in.v(ht * 128, (ht + 1) * 128, 0, c1))
        stg = PTb.v(0, 128, te * 1024, (te + 1) * 1024)
        v_ts(stg, wt_i, scale, None, AL.mult)
        dma("sync", sp_dst.v(ht * 128, (ht + 1) * 128, 0, c1), stg)

    def p3_slot(sa):
        nJ = 8 * (sa + 1)
        for g in range(P3_GROUPS):
            p0 = 2 * g
            pvb = 2048
            rsb = 2560

            def sim(J):
                simc = (J % 2) * 1024
                for pi in range(2):
                    pair = p0 + pi
                    for hf in range(2):
                        oc = simc + hf * 512 + pi * 256
                        mm(PSv(0, 128, oc, oc + 256),
                           kT2.v(64 * hf, 64 * hf + 64, J * 128, (J + 1) * 128),
                           qT.v(64 * hf, 64 * hf + 64,
                                pair * OWN + sa * CH, pair * OWN + (sa + 1) * CH),
                           start=(pi == 0), stop=(pi == 1),
                           tp=(64 * hf, 0))

            def expmask(J):
                pp = J % 2
                simc = pp * 1024
                ptv = etile.v(0, 128, pp * 1024, (pp + 1) * 1024)
                act(ptv, PSv(0, 128, simc, simc + 1024), AF.Exp,
                    bias=(biasT.h[0:128, sa:sa + 1], biasT.v(0, 128, sa, sa + 1)[1]))
                r = J - 8 * sa
                if r >= 0:
                    m_ap = maskS.h[0:128, r * CH:(r + 1) * CH].unsqueeze(1) \
                        .broadcast_to((128, 4, CH))
                    macc = maskS.v(0, 128, r * CH, (r + 1) * CH)[1]
                    pt4 = etile.h[0:128, pp * 1024:(pp + 1) * 1024] \
                        .rearrange("p (a b) -> p a b", a=4)
                    em.add("vector", "c",
                           lambda e, o=pt4, m=m_ap: e.tensor_tensor(o, o, m, AL.mult),
                           ins=[ptv[1], macc], outs=[ptv[1]])

            def pvrs(J):
                pp = J % 2
                for pi in range(2):
                    for hf in range(2):
                        mm(PSv(64 * hf, 64 * hf + 64,
                               pvb + pi * 256, pvb + (pi + 1) * 256),
                           vaug.v(0, 128, J * 64, (J + 1) * 64),
                           etile.v(0, 128, pp * 1024 + hf * 512 + pi * 256,
                                   pp * 1024 + hf * 512 + (pi + 1) * 256),
                           start=(J == 0 and pi == 0), stop=(J == nJ - 1 and pi == 1),
                           tp=(0, 64 * hf))
                for pi in range(2):
                    for hf in range(2):
                        mm(PSv(64 * hf, 64 * hf + 64,
                               rsb + pi * 256, rsb + (pi + 1) * 256),
                           ones64.v(0, 128, 0, 64),
                           etile.v(0, 128, pp * 1024 + hf * 512 + pi * 256,
                                   pp * 1024 + hf * 512 + (pi + 1) * 256),
                           start=(J == 0 and pi == 0), stop=(J == nJ - 1 and pi == 1),
                           tp=(0, 64 * hf))

            sim(0)
            if nJ > 1:
                sim(1)
            for J in range(nJ):
                expmask(J)
                if J >= 1:
                    pvrs(J - 1)
                if J + 2 < nJ:
                    sim(J + 2)
                bg_pump(1)
            pvrs(nJ - 1)
            v_op("reciprocal", scr.v(0, 128, 3072, 3584),
                 PSv(0, 128, rsb, rsb + 512))
            for pi in range(2):
                pair = p0 + pi
                v_tt(oT.v(0, 128, pair * OWN + sa * CH, pair * OWN + (sa + 1) * CH),
                     PSv(0, 128, pvb + pi * 256, pvb + (pi + 1) * 256),
                     scr.v(0, 128, 3072 + pi * 256, 3072 + (pi + 1) * 256),
                     AL.mult)

    for et in range(8 if PHASES >= 5 else 0):
        dma("scalar", woutT.v(0, 128, et * 1024, (et + 1) * 1024),
            sp_wout.v(0, D, et * 128, (et + 1) * 128), transpose=True)

    def p4_tile_bg(cj, tt):
        te = (cj * 4 + tt) % 2
        tok0 = cj * 512 + tt * 128
        a0 = te * 1024
        w0 = 2048
        pieces = []
        pieces.append(lambda: dma(
            "gpsimd", xbuf.v(0, 128, te * 1024, (te + 1) * 1024),
            x_own.v(tok0, tok0 + 128, 0, D)))

        def mmp(cc):
            pc = 3072 + cc * 512
            for et in range(8):
                mm(PSv(0, 128, pc, pc + 512),
                   oT.v(0, 128, et * OWN + tok0, et * OWN + tok0 + 128),
                   woutT.v(0, 128, et * 1024 + cc * 512,
                           et * 1024 + (cc + 1) * 512),
                   start=(et == 0), stop=(et == 7))
        pieces.append(lambda: mmp(0))
        pieces.append(lambda: mmp(1))

        def evac():
            for cc in range(2):
                v_op("tensor_copy",
                     scr.v(0, 128, a0 + cc * 512, a0 + (cc + 1) * 512),
                     PSv(0, 128, 3072 + cc * 512, 3072 + (cc + 1) * 512))
            lnp_stats(scr, a0, te)
        pieces.append(evac)
        pieces.append(lambda: lnp_rstd(te))

        def apply1():
            lnp_apply(scr, a0, scr.v(0, 128, w0, w0 + 1024), te)
            v_tt(scr.v(0, 128, w0, w0 + 1024), scr.v(0, 128, w0, w0 + 1024),
                 lnoG.v(), AL.mult)
            v_tt(scr.v(0, 128, w0, w0 + 1024), scr.v(0, 128, w0, w0 + 1024),
                 lnoB.v(), AL.add)
            lnp_stats(scr, w0, te)
        pieces.append(apply1)
        pieces.append(lambda: lnp_rstd(te))

        def apply2():
            lnp_apply(scr, w0, scr.v(0, 128, a0, a0 + 1024), te)
            v_tt(scr.v(0, 128, a0, a0 + 1024), scr.v(0, 128, a0, a0 + 1024),
                 xbuf.v(0, 128, te * 1024, (te + 1) * 1024), AL.add)
            dma("sync", sp_x1.v(tok0, tok0 + 128, 0, D),
                scr.v(0, 128, a0, a0 + 1024))
            lnp_stats(scr, a0, te)
        pieces.append(apply2)
        pieces.append(lambda: lnp_rstd(te))

        def apply3():
            x1n = PTb.v(0, 128, te * 1024, (te + 1) * 1024)
            lnp_apply(scr, a0, x1n, te)
            dma("sync", sp_x1n.v(tok0, tok0 + 128, 0, D), x1n)
        pieces.append(apply3)
        return pieces

    if PHASES >= 4:
        for ht in range(NHT):
            BG.append(lambda ht=ht: ffn_w_tile(wg_in, sp_wg, ht, DQ, ht))
        for ct in range(8):
            BG.append(lambda ct=ct: dma(
                "scalar", wgT.v(0, 128, ct * HID, (ct + 1) * HID),
                sp_wg.v(0, HID, ct * 128, (ct + 1) * 128), transpose=True))
        def wd_tile(seq):
            ct, part = seq // 3, seq % 3
            te = seq % 2
            wt_ap = xbuf.h.bitcast(I32)[0:128, te * 1024:(te + 1) * 1024]
            wt_i = xbuf.vr(wt_ap, 0, 128, te * 1024, (te + 1) * 1024)
            dma("sync", wt_i,
                wd_in.v(ct * 128, (ct + 1) * 128, part * 1024, (part + 1) * 1024))
            stg = PTb.v(0, 128, te * 1024, (te + 1) * 1024)
            v_ts(stg, wt_i, DQ, None, AL.mult)
            dma("sync",
                sp_wd.v(ct * 128, (ct + 1) * 128, part * 1024, (part + 1) * 1024),
                stg)
        for seq in range(24):
            BG.append(lambda seq=seq: wd_tile(seq))

    for sa in range(min(NSLOT, P3_SLOTS) if PHASES >= 4 else 0):
        em.cur_phase = 3
        p3_slot(sa)
        em.cur_phase = 4
        if PHASES >= 5 and sa % 2 == 1:
            cj = sa // 2
            for tt in range(4):
                for p in p4_tile_bg(cj, tt):
                    BG.append(p)
            for k in range(6):
                ht = cj * 6 + k
                if ht < NHT:
                    BG.append(lambda ht=ht: ffn_w_tile(wu_in, sp_wu, ht, DQ, ht))
    while BG:
        bg_pump(1)

    for ct in range(8 if PHASES >= 6 else 0):
        dma("scalar", wuT.v(0, 128, ct * HID, (ct + 1) * HID),
            sp_wu.v(0, HID, ct * 128, (ct + 1) * 128), transpose=True)
    for ht in range(NHT if PHASES >= 6 else 0):
        dma("scalar", wdT.v(0, 128, ht * 1024, (ht + 1) * 1024),
            sp_wd.v(0, D, ht * 128, (ht + 1) * 128), transpose=True)

    em.cur_phase = 5
    for ck in range(8 if PHASES >= 6 else 0):
        tok0 = ck * CH
        for ct in range(8):
            dma(tq(), PTb.v(0, 128, ct * CH, (ct + 1) * CH),
                sp_x1n.v(tok0, tok0 + CH, ct * 128, (ct + 1) * 128),
                transpose=True)
        for ht in range(NHT):
            hp = (ht % 2) * 1024
            for ct in range(8):
                mm(PSv(0, 128, hp, hp + 256),
                   wgT.v(0, 128, ct * HID + ht * 128, ct * HID + (ht + 1) * 128),
                   PTb.v(0, 128, ct * CH, (ct + 1) * CH),
                   start=(ct == 0), stop=(ct == 7))
            for ct in range(8):
                mm(PSv(0, 128, hp + 512, hp + 768),
                   wuT.v(0, 128, ct * HID + ht * 128, ct * HID + (ht + 1) * 128),
                   PTb.v(0, 128, ct * CH, (ct + 1) * CH),
                   start=(ct == 0), stop=(ct == 7))
            sb = 3072 + (ht % 2) * CH
            act(scr.v(0, 128, sb, sb + CH), PSv(0, 128, hp, hp + 256), AF.Silu)
            v_tt(hsbH.v(0, 128, ht * CH, (ht + 1) * CH),
                 scr.v(0, 128, sb, sb + CH), PSv(0, 128, hp + 512, hp + 768),
                 AL.mult)
        for tt in range(2):
            for cc in range(2):
                pcol = 2048 + tt * 1024 + cc * 512
                for ht in range(NHT):
                    mm(PSv(0, 128, pcol, pcol + 512),
                       hsbH.v(0, 128, ht * CH + tt * 128, ht * CH + tt * 128 + 128),
                       wdT.v(0, 128, ht * 1024 + cc * 512, ht * 1024 + (cc + 1) * 512),
                       start=(ht == 0), stop=(ht == NHT - 1))
        for tt in range(2):
            te = tt % 2
            pcol0 = 2048 + tt * 1024
            v_op("tensor_copy", scr.v(0, 128, te * 2048, te * 2048 + 512),
                 PSv(0, 128, pcol0, pcol0 + 512))
            v_op("tensor_copy", scr.v(0, 128, te * 2048 + 512, te * 2048 + 1024),
                 PSv(0, 128, pcol0 + 512, pcol0 + 1024))
            a0 = te * 2048
            o0 = te * 2048 + 1024
            ln_pass(scr, a0, scr.v(0, 128, o0, o0 + 1024), te)
            dma("gpsimd", xbuf.v(0, 128, te * 1024, (te + 1) * 1024),
                sp_x1.v(tok0 + tt * 128, tok0 + (tt + 1) * 128, 0, D))
            v_tt(scr.v(0, 128, o0, o0 + 1024), scr.v(0, 128, o0, o0 + 1024),
                 xbuf.v(0, 128, te * 1024, (te + 1) * 1024), AL.add)
            dma("sync", y_own.v(tok0 + tt * 128, tok0 + (tt + 1) * 128, 0, D),
                scr.v(0, 128, o0, o0 + 1024))

    if PHASES < 6:
        for t in range(OWN // 128):
            memset(scr.v(0, 128, 0, 1024), 0.0)
            dma("sync", y_own.v(t * 128, (t + 1) * 128, 0, D), scr.v(0, 128, 0, 1024))

    counters = em.resolve()
    sem_keys = sorted({(op.kind, op.eng) for op in em.ops if op.inc})
    by_eng = {}
    for op in em.ops:
        by_eng.setdefault(op.eng, []).append(op)

    sems = {}
    es = contextlib.ExitStack()
    with es:
        for key in sem_keys:
            sems[key] = es.enter_context(nc.semaphore(f"sem_{key[0]}_{key[1]}"))
        block = es.enter_context(nc.Block())

        def make_stream(ops, is_sync):
            def body(e):
                for op in ops:
                    for (key, val) in op.waits:
                        e.wait_ge(sems[key], val)
                    inst = op.emit(e)
                    if op.inc:
                        inst.then_inc(sems[(op.kind, op.eng)],
                                      16 if op.kind == "d" else 1)
                if is_sync:
                    for k in sorted(counters):
                        if k[0] == "d":
                            e.wait_ge(sems[k], counters[k])
            return body

        for eng_name in ("tensor", "vector", "scalar", "gpsimd", "sync"):
            ops = by_eng.get(eng_name, [])
            getattr(block, eng_name)(make_stream(ops, eng_name == "sync"))
    psum_ctx.close()
    return nc


_prog_cache = {}


def _get_program():
    if "nc" not in _prog_cache:
        _prog_cache["nc"] = build_program()
    return _prog_cache["nc"]


def _build_mask(cp):
    m = np.zeros((8, 128, CH), dtype=np.float32)
    j = np.arange(128)[:, None]
    q = np.arange(CH)[None, :]
    for r in range(8):
        if r < 2 * cp:
            m[r] = 1.0
        elif r < 2 * cp + 2:
            p = r - 2 * cp
            m[r] = (128 * p + j <= q).astype(np.float32)
    return np.ascontiguousarray(m.reshape(8 * 128, CH).astype(ml_dtypes.bfloat16))


def kernel(L0_attn_x, L0_attn_w_q, L0_attn_w_kv, L0_attn_w_out,
           L0_attn_ln_g, L0_attn_ln_b, L0_attn_lno_g, L0_attn_lno_b,
           L0_ffn_w_gate, L0_ffn_w_up, L0_ffn_w_down):
    x = np.ascontiguousarray(np.asarray(L0_attn_x, dtype=np.float32))
    wq = np.ascontiguousarray(np.asarray(L0_attn_w_q, dtype=np.float32))
    wkv = np.ascontiguousarray(np.asarray(L0_attn_w_kv, dtype=np.float32))
    wout = np.ascontiguousarray(np.asarray(L0_attn_w_out, dtype=np.float32))
    lngT = np.ascontiguousarray(
        np.asarray(L0_attn_ln_g, dtype=np.float32).reshape(8, 128).T)
    lnbT = np.ascontiguousarray(
        np.asarray(L0_attn_ln_b, dtype=np.float32).reshape(8, 128).T)
    lnog = np.ascontiguousarray(np.broadcast_to(
        np.asarray(L0_attn_lno_g, dtype=np.float32)[None, :], (128, D)))
    lnob = np.ascontiguousarray(np.broadcast_to(
        np.asarray(L0_attn_lno_b, dtype=np.float32)[None, :], (128, D)))
    wg = np.ascontiguousarray(np.asarray(L0_ffn_w_gate, dtype=np.int32))
    wu = np.ascontiguousarray(np.asarray(L0_ffn_w_up, dtype=np.int32))
    wd = np.ascontiguousarray(np.asarray(L0_ffn_w_down, dtype=np.int32))

    nc = _get_program()
    in_maps = []
    for core in range(8):
        b = core // 4
        cp = core % 4
        ids = [4 * a + cp for a in range(NSLOT)]
        xb = np.ascontiguousarray(x[b])
        x_own_np = np.ascontiguousarray(
            np.concatenate([xb[k * CH:(k + 1) * CH] for k in ids], axis=0))
        in_maps.append({
            "x_full": xb,
            "x_own": x_own_np,
            "wq": wq, "wkv": wkv, "wout": wout,
            "lngT": lngT, "lnbT": lnbT, "lnog": lnog, "lnob": lnob,
            "wg": wg, "wu": wu, "wd": wd,
            "maskt": _build_mask(cp),
        })
    res = run_bass_kernel_spmd(nc, in_maps, core_ids=list(range(8)))
    global _last_res
    _last_res = res
    out = np.empty((B, N, D), dtype=np.float32)
    for core in range(8):
        b = core // 4
        cp = core % 4
        yo = np.asarray(res.results[core]["y_own"])
        for a in range(NSLOT):
            k = 4 * a + cp
            out[b, k * CH:(k + 1) * CH] = yo[a * CH:(a + 1) * CH]
    return out
